# version 31
# baseline (speedup 1.0000x reference)
import math
from contextlib import ExitStack

import numpy as np
import ml_dtypes

import concourse.bass as bass
import concourse.mybir as mybir
from concourse.bass_utils import run_bass_kernel_spmd
from concourse.aot_env import allow_ici_all_to_all

F32 = mybir.dt.float32
BF16 = mybir.dt.bfloat16
AF = mybir.ActivationFunctionType
ALU = mybir.AluOpType
AX = mybir.AxisListType

NC = 8
S = 8192
DM = 2048
NH = 16
DH = 128
NF = 8192
TOK = S // NC
SCALE = 1.0 / math.sqrt(DH)
EPS = 1e-5


def build_nc():
    nc = bass.Bass("TRN2", target_bir_lowering=False)

    x_own = nc.dram_tensor("x_own", [TOK, DM], F32, kind="ExternalInput")
    wqkv_own = nc.dram_tensor("wqkv_own", [DM, 6 * DH], F32, kind="ExternalInput")
    wo_in = nc.dram_tensor("wo_in", [DM, DM], F32, kind="ExternalInput")
    w0_own = nc.dram_tensor("w0_own", [NF // NC, DM], F32, kind="ExternalInput")
    b0_in = nc.dram_tensor("b0_in", [NF], F32, kind="ExternalInput")
    w1_own = nc.dram_tensor("w1_own", [DM // NC, NF], F32, kind="ExternalInput")
    b1_in = nc.dram_tensor("b1_in", [DM], F32, kind="ExternalInput")
    out_own = nc.dram_tensor("out_own", [TOK, DM], F32, kind="ExternalOutput")

    xbf_own = nc.dram_tensor("xbf_own", [TOK, DM], BF16)
    xbf_all = nc.dram_tensor("xbf_all", [S, DM], BF16, addr_space="Shared")
    w0bf_own = nc.dram_tensor("w0bf_own", [NF // NC, DM], BF16)
    w0bf = nc.dram_tensor("w0bf", [NF, DM], BF16, addr_space="Shared")
    w1bf_own = nc.dram_tensor("w1bf_own", [DM // NC, NF], BF16)
    w1bf = nc.dram_tensor("w1bf", [DM, NF], BF16, addr_space="Shared")
    b0bf = nc.dram_tensor("b0bf", [64, 128], BF16)
    b1bf = nc.dram_tensor("b1bf", [16, 128], BF16)
    vals_own = nc.dram_tensor("vals_own", [NC * 2 * DH, TOK], BF16)
    vals_a2a = nc.dram_tensor("vals_a2a", [NC * 2 * DH, TOK], BF16)
    w0t_dram = nc.dram_tensor("w0t_dram", [DM, NF], BF16)
    w1t_dram = nc.dram_tensor("w1t_dram", [NF, DM], BF16)
    wobf = nc.dram_tensor("wobf", [DM, DM], BF16)

    ones_col_bf = nc.inline_tensor(np.ones((128, 1), dtype=ml_dtypes.bfloat16), name="c_ocb")
    ones_row_bf = nc.inline_tensor(np.ones((1, 128), dtype=ml_dtypes.bfloat16), name="c_orb")
    ones_col_f = nc.inline_tensor(np.ones((128, 1), dtype=np.float32), name="c_ocf")
    ones_row_f = nc.inline_tensor(np.ones((1, 128), dtype=np.float32), name="c_orf")
    ident_f = nc.inline_tensor(np.eye(128, dtype=np.float32), name="c_idf")
    ident_bf = nc.inline_tensor(np.eye(128, dtype=ml_dtypes.bfloat16), name="c_idb")

    ctx = ExitStack()

    class Ctr:
        def __init__(self, name):
            self.h = ctx.enter_context(nc.semaphore(name))
            self.n = 0

        def inc(self, k=1):
            self.n += k
            return self.n

    GD = Ctr("GD")
    GX = Ctr("GX")
    G0 = Ctr("G0")
    G1 = Ctr("G1")
    GQ = Ctr("GQ")
    GW = Ctr("GW")
    SB = Ctr("SB")
    SXP = [Ctr("SXP0"), Ctr("SXP1")]
    WX = [Ctr("WX0"), Ctr("WX1")]
    WS = [Ctr("WS0"), Ctr("WS1")]
    XO = [Ctr("XO0"), Ctr("XO1")]
    SPJ = [Ctr("SPJ0"), Ctr("SPJ1")]
    SO = [Ctr("SO0"), Ctr("SO1")]
    SW0 = Ctr("SW0")
    SW1 = Ctr("SW1")
    CCX = Ctr("CCX")
    CC0 = Ctr("CC0")
    CC1 = Ctr("CC1")
    CCV = Ctr("CCV")
    CC = Ctr("CC")
    SL = Ctr("SL")
    SX = Ctr("SX")
    SY = Ctr("SY")
    SS = Ctr("SS")
    T = Ctr("T")
    A = Ctr("A")
    V = Ctr("V")

    prog = []

    def em(eng):
        def dec(fn):
            prog.append((eng, fn))
        return dec

    b0_sb = ctx.enter_context(nc.sbuf_tensor("b0_sb", [128, 64], F32))
    b1_sb = ctx.enter_context(nc.sbuf_tensor("b1_sb", [128, 16], F32))
    b0t = ctx.enter_context(nc.sbuf_tensor("b0t", [128, 64], BF16))
    b1t = ctx.enter_context(nc.sbuf_tensor("b1t", [128, 16], BF16))
    oc_bf = ctx.enter_context(nc.sbuf_tensor("oc_bf", [128, 1], BF16))
    or_bf = ctx.enter_context(nc.sbuf_tensor("or_bf", [1, 128], BF16))
    oc_f = ctx.enter_context(nc.sbuf_tensor("oc_f", [128, 1], F32))
    or_f = ctx.enter_context(nc.sbuf_tensor("or_f", [1, 128], F32))
    id_f = ctx.enter_context(nc.sbuf_tensor("id_f", [128, 128], F32))
    id_bf = ctx.enter_context(nc.sbuf_tensor("id_bf", [128, 128], BF16))

    @em("g")
    def _(g):
        g.dma_start(out=xbf_own.ap(), in_=x_own.ap()).then_inc(GX.h, 16)
        g.dma_start(out=b0bf.ap(), in_=b0_in.ap().rearrange("(a b) -> a b", b=128)).then_inc(GD.h, 16)
        g.dma_start(out=b1bf.ap(), in_=b1_in.ap().rearrange("(a b) -> a b", b=128)).then_inc(GD.h, 16)
    GX.inc(16)
    casts_gd = GD.inc(32)

    def _emit_w_casts():
        @em("g")
        def _(g):
            g.wait_ge(SXP[1].h, 512)
            g.dma_start(out=w0bf_own.ap(), in_=w0_own.ap()).then_inc(G0.h, 16)
            g.dma_start(out=w1bf_own.ap(), in_=w1_own.ap()).then_inc(G1.h, 16)
            g.dma_start(out=wobf.ap(), in_=wo_in.ap()).then_inc(GD.h, 16)
        G0.inc(16); G1.inc(16)
        return GD.inc(16)

    @em("s")
    def _(s, w=casts_gd):
        s.wait_ge(GD.h, w)
        s.dma_start(out=b0t[:, :], in_=b0bf.ap(), transpose=True).then_inc(SB.h, 16)
        s.dma_start(out=b1t[:, :], in_=b1bf.ap(), transpose=True).then_inc(SB.h, 16)
    bias_sl = SB.inc(32)

    @em("v")
    def _(v, w=bias_sl):
        v.wait_ge(SB.h, w)
        v.tensor_copy(b0_sb[:, :], b0t[:, :])
        v.tensor_copy(b1_sb[:, :], b1t[:, :])
        v.drain().then_inc(V.h, 1)
    bias_v = V.inc()

    def _emit_ag_x():
        @em("g")
        def _(g):
            g.wait_ge(GX.h, 16)
            g.collective_compute("AllGather", ALU.bypass, replica_groups=[list(range(NC))],
                                 ins=[xbf_own.ap().opt()],
                                 outs=[xbf_all.ap().opt()]).then_inc(CCX.h, 1)
        return CCX.inc()

    def _emit_ag_w0(wait_loads=None):
        @em("g")
        def _(g, wait_loads=wait_loads):
            g.wait_ge(G0.h, 16)
            if wait_loads is not None:
                g.wait_ge(wait_loads[0].h, wait_loads[1])
            g.collective_compute("AllGather", ALU.bypass, replica_groups=[list(range(NC))],
                                 ins=[w0bf_own.ap().opt()], outs=[w0bf.ap().opt()]).then_inc(CC0.h, 1)
        return CC0.inc()

    def _emit_ag_w1():
        @em("g")
        def _(g):
            g.wait_ge(G1.h, 16)
            g.collective_compute("AllGather", ALU.bypass, replica_groups=[list(range(NC))],
                                 ins=[w1bf_own.ap().opt()], outs=[w1bf.ap().opt()]).then_inc(CC1.h, 1)
        return CC1.inc()

    @em("s")
    def _(s):
        s.dma_start(out=oc_bf[:, :], in_=ones_col_bf.ap()).then_inc(SL.h, 16)
        s.dma_start(out=or_bf[:, :], in_=ones_row_bf.ap()).then_inc(SL.h, 16)
        s.dma_start(out=oc_f[:, :], in_=ones_col_f.ap()).then_inc(SL.h, 16)
        s.dma_start(out=or_f[:, :], in_=ones_row_f.ap()).then_inc(SL.h, 16)
        s.dma_start(out=id_f[:, :], in_=ident_f.ap()).then_inc(SL.h, 16)
        s.dma_start(out=id_bf[:, :], in_=ident_bf.ap()).then_inc(SL.h, 16)
    consts_sl = SL.inc(16 * 6)

    scope_b = ExitStack()
    qkv_sb = scope_b.enter_context(nc.sbuf_tensor("qkv_sb", [128, 6 * S], BF16))
    psA = scope_b.enter_context(nc.psum_tensor("psA", [128, 1536], F32))
    psB = scope_b.enter_context(nc.psum_tensor("psB", [128, 2048], F32))
    c_ps = scope_b.enter_context(nc.psum_tensor("c_ps", [128, 320], F32))

    def qt(h, wi):
        o = (h * 3 + wi) * S
        return qkv_sb[:, o:o + S]

    scope_b1 = ExitStack()
    wq_sb = scope_b1.enter_context(nc.sbuf_tensor("wq_sb", [128, 6 * 16 * 128], BF16))
    sqb = scope_b1.enter_context(nc.sbuf_tensor("sqb", [128, 2048], BF16))
    NSLOT = 20
    xF = scope_b1.enter_context(nc.sbuf_tensor("xF", [128, NSLOT * 2048], BF16))

    def slot_of(ts2, dc):
        return (ts2 * 16 + dc) % NSLOT

    @em("g")
    def _(g):
        g.dma_start(out=wq_sb[:, :].rearrange("p (ct dc col) -> p ct dc col", ct=6, dc=16),
                    in_=wqkv_own.ap().rearrange("(dc p) (ct col) -> p ct dc col", p=128, col=128)
                    ).then_inc(GQ.h, 16)
    wq_gd = GQ.inc(16)

    cc_x = _emit_ag_x()
    wobf_gd = _emit_w_casts()
    cc_w0 = _emit_ag_w0(wait_loads=(SXP[1], 512))
    cc_w1 = _emit_ag_w1()

    def wqt(h, wi, dc):
        o = ((h * 3 + wi) * 16 + dc) * 128
        return wq_sb[:, o:o + 128]

    series_T = []
    series_V = []
    rel_t = {}
    load_sl = {}
    SXK = SXP

    NORM_BASE = {0: 64, 1: 128, 3: 192, 4: 256}
    norm_pending = []
    norm_entry = 0
    norm_mm_t = []
    series_serial = 0

    def flush_norms(upto_serial):
        while norm_pending and norm_pending[0][0] <= upto_serial:
            _, sq_v, colbase, tsv, sh = norm_pending.pop(0)

            @em("t")
            def _(t, sq_v=sq_v, colbase=colbase, tsv=tsv, sh=sh):
                t.wait_ge(V.h, sq_v)
                for j in range(8):
                    col = colbase + tsv * 8 + j
                    ins = t.matmul(c_ps[:, col:col + 1],
                                   sqb[:, sh * 1024 + j * 128:sh * 1024 + (j + 1) * 128],
                                   oc_bf[:, :], start=True, stop=True)
                ins.then_inc(T.h, 1)
            norm_mm_t.append(T.inc())

    for ts2 in range(4):
        sem_x = SXK[ts2 % 2]
        for dc in range(16):
            slot = slot_of(ts2, dc)
            prev_use = None
            for back in (1, 2):
                tp = ts2 - back
                if tp < 0:
                    break
                off = slot - (tp * 16) % NSLOT
                dcp = off % NSLOT
                if 0 <= dcp < 16:
                    prev_use = (tp, dcp)
                    break

            @em("s")
            def _(s, ts2=ts2, dc=dc, slot=slot, sem_x=sem_x, wprev=sem_x.n,
                  wrel=(rel_t.get(prev_use) if prev_use else None)):
                s.wait_ge(CCX.h, 1)
                if wrel is not None:
                    s.wait_ge(T.h, wrel)
                if dc >= 1:
                    s.wait_ge(sem_x.h, wprev)
                s.dma_start(out=xF[:, slot * 2048:(slot + 1) * 2048],
                            in_=xbf_all.ap()[ts2 * 2048:(ts2 + 1) * 2048, dc * 128:(dc + 1) * 128],
                            transpose=True).then_inc(sem_x.h, 16)
            sem_x.inc(16)
        load_sl[ts2] = sem_x.n

        for half in range(2):
            ts = ts2 * 2 + half
            release = (half == 1)
            for ci in range(6):
                flush_norms(series_serial - 2)
                series_serial += 1
                hh, wi = divmod(ci, 3)
                sidx = ts * 6 + ci
                ps = (psA, psB)[ci % 2]
                w_evac = series_V[sidx - 2] if sidx >= 2 else None
                rel_this = release and ci == 5 and ts2 < 3
                if rel_this:
                    for dc in range(16):
                        rel_t[(ts2, dc)] = T.n + dc + 1
                    T.n += 16
                else:
                    T.n += 1

                first_series = (half == 0 and ci == 0)

                @em("t")
                def _(t, hh=hh, wi=wi, ts=ts, ts2=ts2, half=half, ps=ps, ci=ci,
                      sem_x=sem_x, xf_sl=load_sl[ts2], w_evac=w_evac, wq_gd=wq_gd,
                      wc=consts_sl, rel_this=rel_this, first_series=first_series):
                    t.wait_ge(SL.h, wc)
                    if not first_series:
                        t.wait_ge(sem_x.h, xf_sl)
                    t.wait_ge(GQ.h, wq_gd)
                    if w_evac is not None:
                        t.wait_ge(V.h, w_evac)
                    for ck in range(2):
                        for dc in range(16):
                            if first_series and ck == 0:
                                t.wait_ge(sem_x.h, xf_sl - 256 + 16 * (dc + 1))
                            base = slot_of(ts2, dc) * 2048 + half * 1024 + ck * 512
                            ins = t.matmul(ps[:, ck * 512:(ck + 1) * 512], wqt(hh, wi, dc),
                                           xF[:, base:base + 512],
                                           start=(dc == 0), stop=(dc == 15))
                            if rel_this and ck == 1:
                                ins.then_inc(T.h, 1)
                    if not rel_this:
                        ins.then_inc(T.h, 1)
                series_T.append(T.n)

                @em("v")
                def _(v, hh=hh, wi=wi, ts=ts, ps=ps, w=series_T[-1]):
                    v.wait_ge(T.h, w)
                    v.tensor_copy(qt(hh, wi)[:, ts * 1024:(ts + 1) * 1024], ps[:, 0:1024])
                    v.drain().then_inc(V.h, 1)
                series_V.append(V.inc())

                if ci in NORM_BASE:
                    sq_half = norm_entry % 2
                    wmm = norm_mm_t[norm_entry - 2] if norm_entry >= 2 else None

                    @em("v")
                    def _(v, hh=hh, wi=wi, ts=ts, sh=sq_half, wmm=wmm):
                        if wmm is not None:
                            v.wait_ge(T.h, wmm)
                        src = qt(hh, wi)[:, ts * 1024:(ts + 1) * 1024]
                        v.tensor_tensor(sqb[:, sh * 1024:(sh + 1) * 1024], src, src, ALU.mult)
                        v.drain().then_inc(V.h, 1)
                    norm_pending.append((series_serial, V.inc(), NORM_BASE[ci], ts, sq_half))
                    norm_entry += 1

    flush_norms(series_serial)
    norms_done_t = T.n
    qkv_done_v = V.n
    qkv_done_t = T.n


    scope_b1.close()
    scope_b2 = ExitStack()
    p_sb = [scope_b2.enter_context(nc.sbuf_tensor(f"p_sb{i}", [128, S], BF16)) for i in range(3)]
    vals_sb = scope_b2.enter_context(nc.sbuf_tensor("vals_sb", [128, S], BF16))
    cRow = scope_b2.enter_context(nc.sbuf_tensor("cRow", [1, S // 128 * 128], BF16))
    negB = scope_b2.enter_context(nc.sbuf_tensor("negB", [128, 128], F32))
    zparts = scope_b2.enter_context(nc.sbuf_tensor("zparts", [128, 24], F32))
    z1 = scope_b2.enter_context(nc.sbuf_tensor("z1", [128, 1], F32))
    zr = scope_b2.enter_context(nc.sbuf_tensor("zr", [128, 1], F32))
    zrb = [scope_b2.enter_context(nc.sbuf_tensor(f"zrb{i}", [128, 1], BF16)) for i in range(2)]
    zpad = scope_b2.enter_context(nc.sbuf_tensor("zpad", [128, 1], F32))
    wtb = [scope_b2.enter_context(nc.sbuf_tensor(f"wtb{i}", [128, 2048], BF16)) for i in range(2)]
    c_sb = scope_b2.enter_context(nc.sbuf_tensor("c_sb", [128, 64], F32))
    nrmH = scope_b2.enter_context(nc.sbuf_tensor("nrmH", [128, 256], F32))
    mk1 = scope_b2.enter_context(nc.sbuf_tensor("mk1", [128, 1], F32))
    mk2 = scope_b2.enter_context(nc.sbuf_tensor("mk2", [1, 128], F32))
    mks = scope_b2.enter_context(nc.sbuf_tensor("mks", [1, 1], F32))
    mksb = scope_b2.enter_context(nc.sbuf_tensor("mksb", [1, 1], BF16))
    scl = scope_b2.enter_context(nc.sbuf_tensor("scl", [128, 2], F32))

    wt_ss = []
    for dc in range(16):
        for ns in range(4):
            @em("s")
            def _(s, dc=dc, ns=ns, wc=cc_w0, i=len(wt_ss), wq=qkv_done_t):
                s.wait_ge(CC0.h, wc)
                s.wait_ge(T.h, wq)
                if i >= 2:
                    s.wait_ge(WS[i % 2].h, wt_ss[i - 2])
                s.dma_start(out=wtb[i % 2][:, :],
                            in_=w0bf.ap()[ns * 2048:(ns + 1) * 2048, dc * 128:(dc + 1) * 128],
                            transpose=True).then_inc(WX[i % 2].h, 16)
            cur_sx = WX[len(wt_ss) % 2].inc(16)

            @em("s")
            def _(s, dc=dc, ns=ns, w=cur_sx, i=len(wt_ss)):
                s.wait_ge(WX[i % 2].h, w)
                s.dma_start(out=w0t_dram.ap()[dc * 128:(dc + 1) * 128, ns * 2048:(ns + 1) * 2048],
                            in_=wtb[i % 2][:, :]).then_inc(WS[i % 2].h, 16)
            wt_ss.append(WS[len(wt_ss) % 2].inc(16))
    for ntc in range(64):
        @em("s")
        def _(s, ntc=ntc, wc=cc_w1, i=len(wt_ss)):
            s.wait_ge(CC1.h, wc)
            s.wait_ge(WS[i % 2].h, wt_ss[i - 2])
            s.dma_start(out=wtb[i % 2][:, :],
                        in_=w1bf.ap()[:, ntc * 128:(ntc + 1) * 128],
                        transpose=True).then_inc(WX[i % 2].h, 16)
        cur_sx = WX[len(wt_ss) % 2].inc(16)

        @em("s")
        def _(s, ntc=ntc, w=cur_sx, i=len(wt_ss)):
            s.wait_ge(WX[i % 2].h, w)
            s.dma_start(out=w1t_dram.ap()[ntc * 128:(ntc + 1) * 128, :],
                        in_=wtb[i % 2][:, :]).then_inc(WS[i % 2].h, 16)
        wt_ss.append(WS[len(wt_ss) % 2].inc(16))
    wt_done = (WS[0].n, WS[1].n)

    vals_store_ss = []

    @em("v")
    def _(v, w=norms_done_t):
        v.wait_ge(T.h, w)
        v.tensor_copy(nrmH[:, :], c_ps[:, 64:320])
        v.drain().then_inc(V.h, 1)
    nev = V.inc()

    ready_vh = []
    prev_tr_t = None
    for ph in range(2):
        @em("v")
        def _(v, ph=ph, wtr=prev_tr_t):
            if wtr is not None:
                v.wait_ge(T.h, wtr)
            v.reduce_max(mk1[:, :], nrmH[:, ph * 128 + 64:ph * 128 + 128], axis=AX.X)
            v.drain().then_inc(V.h, 1)
        mkv = V.inc()

        @em("t")
        def _(t, w=mkv, wc=consts_sl):
            t.wait_ge(V.h, w)
            t.wait_ge(SL.h, wc)
            t.transpose(psB[0:1, 0:128], mk1[:, :], id_f[:, :]).then_inc(T.h, 1)
        trt = T.inc()
        prev_tr_t = trt

        @em("v")
        def _(v, w=trt):
            v.wait_ge(T.h, w)
            v.tensor_copy(mk2[:, :], psB[0:1, 0:128])
            v.drain()
            v.reduce_max(mks[:, :], mk2[:, :], axis=AX.X)
            v.drain()
            v.tensor_scalar(mks[:, :], mks[:, :], float(1.0404 / DH), None, ALU.mult)
            v.drain()
            v.tensor_copy(mksb[:, :], mks[:, :])
            v.drain().then_inc(V.h, 1)
        mksv = V.inc()

        @em("t")
        def _(t, w=mksv):
            t.wait_ge(V.h, w)
            t.matmul(psB[:, 128:129], or_bf[:, :], mksb[:, :], start=True, stop=True).then_inc(T.h, 1)
        bct = T.inc()

        @em("v")
        def _(v, ph=ph, w=bct):
            v.wait_ge(T.h, w)
            v.tensor_copy(scl[:, ph:ph + 1], psB[:, 128:129])
            v.drain().then_inc(V.h, 1)
        sclv = V.inc()

        @em("a")
        def _(a, ph=ph, w=sclv):
            a.wait_ge(V.h, w)
            a.activation(negB[:, ph * 64:(ph + 1) * 64], nrmH[:, ph * 128:ph * 128 + 64],
                         AF.Sqrt, bias=0.0, scale=scl[:, ph:ph + 1])
            a.drain().then_inc(A.h, 1)
        sqa = A.inc()

        @em("v")
        def _(v, ph=ph, w=sqa):
            v.wait_ge(A.h, w)
            v.tensor_scalar(negB[:, ph * 64:(ph + 1) * 64], negB[:, ph * 64:(ph + 1) * 64],
                            -1.0, None, ALU.mult)
            v.drain().then_inc(V.h, 1)
        ready_vh.append(V.inc())

    def head_b(h, ready_v, prev):
        PASS_EVEN = [(0, 1536), (1, 2048), (0, 1536), (1, 2048), (0, 896), (0, 128)]
        PASS_ODD = [(1, 2048), (0, 1536), (1, 2048), (0, 1536), (1, 896), (1, 128)]

        def passes_of(qb):
            plist = PASS_EVEN if qb % 2 == 0 else PASS_ODD
            out = []
            k0 = 0
            for tile, klen in plist:
                out.append((tile, k0, klen))
                k0 += klen
            return out

        def tile_passes(qb, tile):
            return [i for i, (t, _, _) in enumerate(passes_of(qb)) if t == tile]

        last_mm = {}
        exp_a = {}
        zr_v = {}
        col_t = {}

        PREV = {0: (-1, 3), 1: (-1, 5), 2: (0, 0), 3: (0, 1), 4: (0, 2), 5: (0, 2)}

        def emit_scores(qb, p):
            pl = passes_of(qb)
            tile, k0, klen = pl[p]
            ps = (psA, psB)[tile]
            off = 0 if p < 5 else 896
            dq, dp = PREV[p]
            prevp = (qb + dq, dp)
            waits = [("V", ready_v)]
            if prevp in exp_a:
                waits.append(("A", exp_a[prevp]))
            if qb == 0 and p == 0 and prev is not None:
                waits.append(("V", prev["done"]))

            @em("t")
            def _(t, h=h, qb=qb, ps=ps, k0=k0, klen=klen, off=off, waits=tuple(waits)):
                for kind, w in waits:
                    t.wait_ge({"V": V.h, "A": A.h}[kind], w)
                for j in range((klen + 511) // 512):
                    w0 = j * 512
                    w1 = min(klen, w0 + 512)
                    ins = t.matmul(ps[:, off + w0:off + w1],
                                   qt(h, 0)[:, qb * 128:(qb + 1) * 128],
                                   qt(h, 1)[:, k0 + w0:k0 + w1],
                                   start=True, stop=True)
                ins.then_inc(T.h, 1)
            last_mm[(qb, p)] = T.inc()

            pw = []
            if p == 0 and (qb - 4) in zr_v:
                pw.append(("V", zr_v[qb - 4]))
            if p == 0 and (qb - 3) in col_t:
                pw.append(("T", col_t[qb - 3]))
            if qb == 0 and p == 0 and prev is not None:
                pw.append(("V", prev["zr63"]))
                pw.append(("T", prev["col63"]))

            @em("a")
            def _(a, h=h, qb=qb, p=p, ps=ps, k0=k0, klen=klen, off=off,
                  w=last_mm[(qb, p)], pw=tuple(pw)):
                a.wait_ge(T.h, w)
                for kind, wv in pw:
                    a.wait_ge({"V": V.h, "T": T.h}[kind], wv)
                zo = (qb % 4) * 6 + p
                ins = a.activation(p_sb[qb % 3][:, k0:k0 + klen], ps[:, off:off + klen],
                                   AF.Exp, bias=negB[:, h * 64 + qb:h * 64 + qb + 1],
                                   scale=float(SCALE),
                                   accum_out=zparts[:, zo:zo + 1])
                if p == 5 and qb % 2 == 1:
                    a.drain().then_inc(A.h, 1)
                else:
                    ins.then_inc(A.h, 1)
            exp_a[(qb, p)] = A.inc()

        def emit_zr(qb, wa):
            if qb >= 2:
                wcol = col_t.get(qb - 2)
            else:
                wcol = prev[("col62", "col63")[qb]] if prev is not None else None

            @em("v")
            def _(v, qb=qb, w=wa, wcol=wcol):
                v.wait_ge(A.h, w)
                if wcol is not None:
                    v.wait_ge(T.h, wcol)
                v.reduce_sum(z1[:, :], zparts[:, (qb % 4) * 6:(qb % 4) * 6 + 6], axis=AX.X)
                v.drain()
                v.reciprocal(zr[:, :], z1[:, :])
                v.drain()
                v.tensor_copy(zrb[qb % 2][:, :], zr[:, :])
                v.drain().then_inc(V.h, 1)
            zr_v[qb] = V.inc()

        def emit_colsum_half(qb, hf):
            @em("t")
            def _(t, h=h, qb=qb, hf=hf, w=zr_v[qb],
                  wcev=(prev["cev"] if (qb == 0 and hf == 0 and prev is not None) else None)):
                t.wait_ge(V.h, w)
                if wcev is not None:
                    t.wait_ge(V.h, wcev)
                for j in range(hf * 32, hf * 32 + 32):
                    ins = t.matmul(c_ps[:, j:j + 1], p_sb[qb % 3][:, j * 128:(j + 1) * 128],
                                   zrb[qb % 2][:, :],
                                   start=(qb == 0 and j == 0), stop=(qb == 63 and j == 63),
                                   skip_group_check=True)
                ins.then_inc(T.h, 1)
            return T.inc()

        def emit_colsum(qb, hf=None):
            if hf is None:
                emit_colsum_half(qb, 0)
                col_t[qb] = emit_colsum_half(qb, 1)
            else:
                r = emit_colsum_half(qb, hf)
                if hf == 1:
                    col_t[qb] = r

        for qb in range(64):
            emit_scores(qb, 0)
            emit_scores(qb, 1)
            if qb >= 2:
                emit_colsum(qb - 2, 0)
            emit_scores(qb, 2)
            emit_scores(qb, 3)
            if qb >= 2:
                emit_colsum(qb - 2, 1)
            emit_scores(qb, 4)
            emit_scores(qb, 5)
            if qb % 2 == 1:
                emit_zr(qb - 1, exp_a[(qb, 5)])
                emit_zr(qb, exp_a[(qb, 5)])
        emit_colsum(62)
        emit_colsum(63)

        @em("v")
        def _(v, w=col_t[63]):
            v.wait_ge(T.h, w)
            v.tensor_copy(c_sb[:, :], c_ps[:, 0:64])
            v.drain().then_inc(V.h, 1)
        cev_v = {63: V.inc()}

        cbf_v = cev_v[63]

        cev_list = []
        for grp4 in range(4):
            @em("t")
            def _(t, grp4=grp4, w=cbf_v, wprev=(cev_list[grp4 - 1] if grp4 >= 1 else None)):
                t.wait_ge(V.h, w)
                if wprev is not None:
                    t.wait_ge(V.h, wprev)
                for j in range(16):
                    col = grp4 * 16 + j
                    ins = t.transpose(psB[0:1, j * 128:(j + 1) * 128],
                                      c_sb[:, col:col + 1], id_f[:, :])
                ins.then_inc(T.h, 1)
            tcur = T.inc()

            @em("v")
            def _(v, grp4=grp4, w=tcur):
                v.wait_ge(T.h, w)
                v.tensor_copy(cRow[0:1, grp4 * 2048:(grp4 + 1) * 2048], psB[0:1, 0:2048])
                v.drain().then_inc(V.h, 1)
            cev_list.append(V.inc())
        cRow_v = V.n

        vm_last = []
        for grp in range(16):
            ps = (psA, psB)[grp % 2]

            @em("t")
            def _(t, grp=grp, ps=ps, w=cRow_v, wprev=(vm_last[grp - 2] if grp >= 2 else None)):
                t.wait_ge(V.h, w)
                if wprev is not None:
                    t.wait_ge(V.h, wprev)
                t.matmul(ps[:, 512:1024], or_bf[:, :],
                         cRow[0:1, grp * 512:(grp + 1) * 512], start=True, stop=True).then_inc(T.h, 1)
            bct2 = T.inc()

            wss = vals_store_ss[0] if (h == 1 and grp == 0) else None

            @em("v")
            def _(v, h=h, grp=grp, ps=ps, w=bct2, wss=wss):
                v.wait_ge(T.h, w)
                if wss is not None:
                    v.wait_ge(SS.h, wss)
                v.tensor_tensor(vals_sb[:, grp * 512:(grp + 1) * 512],
                                qt(h, 2)[:, grp * 512:(grp + 1) * 512], ps[:, 512:1024], ALU.mult)
                v.drain().then_inc(V.h, 1)
            vm_last.append(V.inc())

        @em("s")
        def _(s, h=h, w=vm_last[-1]):
            s.wait_ge(V.h, w)
            dst = vals_own.ap().rearrange("(sh r) t -> sh r t", sh=NC)
            dst = dst[:, h * 128:(h + 1) * 128, :].rearrange("sh p t -> p sh t")
            s.dma_start(out=dst, in_=vals_sb[:, :].rearrange("p (sh t) -> p sh t", sh=NC)
                        ).then_inc(SS.h, 16)
        vals_store_ss.append(SS.inc(16))
        return {"done": V.n, "col62": col_t[62], "col63": col_t[63],
                "zr63": zr_v[63], "cev": cev_v[63]}

    ctx0 = head_b(0, ready_vh[1], None)
    head_b(1, ready_vh[1], ctx0)

    scope_b2.close()
    scope_b.close()

    scope_c = ExitStack()
    r1 = scope_c.enter_context(nc.sbuf_tensor("r1", [128, 16 * 1024], F32))
    mu_b = scope_c.enter_context(nc.sbuf_tensor("mu_b", [128, 1024], F32))
    rs_b = scope_c.enter_context(nc.sbuf_tensor("rs_b", [128, 1024], F32))
    row_mu = scope_c.enter_context(nc.sbuf_tensor("row_mu", [1, 1024], F32))
    row_v = scope_c.enter_context(nc.sbuf_tensor("row_v", [1, 1024], F32))
    row_t = scope_c.enter_context(nc.sbuf_tensor("row_t", [1, 1024], F32))
    sq_buf = [scope_c.enter_context(nc.sbuf_tensor(f"sq{i}", [128, 1024], F32)) for i in range(2)]
    pC = [scope_c.enter_context(nc.psum_tensor(f"pC{i}", [128, 1024], F32)) for i in range(4)]

    scope_c1 = ExitStack()
    wo_sb = scope_c1.enter_context(nc.sbuf_tensor("wo_sb", [128, 16 * 2048], BF16))

    @em("g")
    def _(g, w=vals_store_ss[1], wt=wt_done, wc=wobf_gd):
        g.wait_ge(SS.h, w)
        g.wait_ge(GD.h, wc)
        g.wait_ge(WS[0].h, wt[0])
        g.wait_ge(WS[1].h, wt[1])
        g.dma_start(out=wo_sb[:, :].rearrange("p (dc o) -> p dc o", dc=16),
                    in_=wobf.ap().rearrange("(dc p) o -> p dc o", p=128)).then_inc(GW.h, 16)
    wo_gd = GW.inc(16)

    @em("g")
    def _(g, w=vals_store_ss[1]):
        g.wait_ge(SS.h, w)
        with allow_ici_all_to_all():
            g.collective_compute("AllToAll", ALU.bypass, replica_groups=[list(range(NC))],
                                 ins=[vals_own.ap().opt()], outs=[vals_a2a.ap().opt()]).then_inc(CCV.h, 1)
    cc_vals = CCV.inc()

    valsF = scope_c1.enter_context(nc.sbuf_tensor("valsF", [128, 16 * 1024], BF16))
    xoF = [scope_c1.enter_context(nc.sbuf_tensor(f"xoF{i}", [128, 1024], BF16)) for i in range(2)]

    @em("s")
    def _(s, w=cc_vals, wt=wt_done):
        s.wait_ge(CCV.h, w)
        s.wait_ge(WS[0].h, wt[0])
        s.wait_ge(WS[1].h, wt[1])
        s.dma_start(out=valsF[:, :].rearrange("p (hd t) -> p hd t", hd=16),
                    in_=vals_a2a.ap().rearrange("(hd p) t -> p hd t", p=128)).then_inc(SL.h, 16)
    valsF_sl = SL.inc(16)

    xo_sl = {}

    def emit_xo_load(dc, wait_v):
        @em("s")
        def _(s, dc=dc, wait_v=wait_v):
            s.wait_ge(GX.h, 16)
            if wait_v is not None:
                s.wait_ge(V.h, wait_v)
            s.dma_start(out=xoF[dc % 2][:, :], in_=xbf_own.ap()[:, dc * 128:(dc + 1) * 128],
                        transpose=True).then_inc(XO[dc % 2].h, 16)
        xo_sl[dc] = XO[dc % 2].inc(16)

    emit_xo_load(0, None)
    emit_xo_load(1, None)

    r1_v = {}
    for o in range(16):
        ps = pC[o % 2]

        @em("t")
        def _(t, o=o, ps=ps, wv=valsF_sl, wg=wo_gd, wx=xo_sl[o],
              wfree=(r1_v[o - 2] if o >= 2 else None)):
            t.wait_ge(SL.h, wv)
            t.wait_ge(GW.h, wg)
            t.wait_ge(XO[o % 2].h, wx)
            if wfree is not None:
                t.wait_ge(V.h, wfree)
            for ck in range(2):
                for hd in range(16):
                    ins = t.matmul(ps[:, ck * 512:(ck + 1) * 512],
                                   wo_sb[:, (hd * 16 + o) * 128:(hd * 16 + o) * 128 + 128],
                                   valsF[:, hd * 1024 + ck * 512:hd * 1024 + (ck + 1) * 512],
                                   start=(hd == 0), stop=(hd == 15))
            ins.then_inc(T.h, 1)
        opt = T.inc()

        @em("v")
        def _(v, o=o, ps=ps, w=opt):
            v.wait_ge(T.h, w)
            v.scalar_tensor_tensor(r1[:, o * 1024:(o + 1) * 1024], ps[:, :], 1.0,
                                   xoF[o % 2][:, :], ALU.mult, ALU.add)
            v.drain().then_inc(V.h, 1)
        r1_v[o] = V.inc()
        if o + 2 < 16:
            emit_xo_load(o + 2, r1_v[o])
    r1_done = V.n

    scope_c1.close()
    scope_ln = ExitStack()
    ln1bf = scope_ln.enter_context(nc.sbuf_tensor("ln1bf", [128, 16 * 1024], BF16))

    def layer_norm(wait_v, out_bf, seg_waits=None):
        def segw(dc):
            return seg_waits[dc] if seg_waits is not None else wait_v

        @em("t")
        def _(t, sw=tuple(segw(dc) for dc in range(16))):
            for ck in range(2):
                for dc in range(16):
                    if ck == 0:
                        t.wait_ge(V.h, sw[dc])
                    ins = t.matmul(pC[2][0:1, ck * 512:(ck + 1) * 512], oc_f[:, :],
                                   r1[:, dc * 1024 + ck * 512:dc * 1024 + (ck + 1) * 512],
                                   start=(dc == 0), stop=(dc == 15))
            ins.then_inc(T.h, 1)
        sums_t = T.inc()

        sqmm = []
        for dc in range(16):
            @em("a")
            def _(a, dc=dc, w=segw(dc), wp=(sqmm[dc - 2] if dc >= 2 else None)):
                a.wait_ge(V.h, w)
                if wp is not None:
                    a.wait_ge(T.h, wp)
                a.activation(sq_buf[dc % 2][:, :], r1[:, dc * 1024:(dc + 1) * 1024], AF.Square)
                a.drain().then_inc(A.h, 1)
            sqa = A.inc()

            @em("t")
            def _(t, dc=dc, w=sqa):
                t.wait_ge(A.h, w)
                for ck in range(2):
                    ins = t.matmul(pC[3][0:1, ck * 512:(ck + 1) * 512], oc_f[:, :],
                                   sq_buf[dc % 2][:, ck * 512:(ck + 1) * 512],
                                   start=(dc == 0), stop=(dc == 15))
                ins.then_inc(T.h, 1)
            sqmm.append(T.inc())
        sq_t = sqmm[-1]

        @em("v")
        def _(v, w1=sums_t, w2=sq_t):
            v.wait_ge(T.h, w1)
            v.wait_ge(T.h, w2)
            v.tensor_scalar(row_mu[:, :], pC[2][0:1, 0:1024], float(1.0 / DM), None, ALU.mult)
            v.tensor_scalar(row_v[:, :], pC[3][0:1, 0:1024], float(1.0 / DM), None, ALU.mult)
            v.drain()
            v.tensor_tensor(row_t[:, :], row_mu[:, :], row_mu[:, :], ALU.mult)
            v.drain()
            v.tensor_tensor(row_v[:, :], row_v[:, :], row_t[:, :], ALU.subtract)
            v.drain()
            v.tensor_scalar(row_v[:, :], row_v[:, :], float(EPS), None, ALU.add)
            v.drain().then_inc(V.h, 1)
        var_v = V.inc()

        @em("a")
        def _(a, w=var_v):
            a.wait_ge(V.h, w)
            a.activation(row_t[:, :], row_v[:, :], AF.Sqrt, bias=0.0, scale=1.0)
            a.drain().then_inc(A.h, 1)
        std_a = A.inc()

        @em("v")
        def _(v, w=std_a):
            v.wait_ge(A.h, w)
            v.reciprocal(row_v[:, :], row_t[:, :])
            v.drain().then_inc(V.h, 1)
        rstd_v = V.inc()

        @em("t")
        def _(t, w=rstd_v):
            t.wait_ge(V.h, w)
            for ck in range(2):
                t.matmul(pC[2][:, ck * 512:(ck + 1) * 512], or_f[:, :],
                         row_mu[0:1, ck * 512:(ck + 1) * 512], start=True, stop=True)
            for ck in range(2):
                ins = t.matmul(pC[3][:, ck * 512:(ck + 1) * 512], or_f[:, :],
                               row_v[0:1, ck * 512:(ck + 1) * 512], start=True, stop=True)
            ins.then_inc(T.h, 1)
        bc_t = T.inc()

        @em("v")
        def _(v, w=bc_t):
            v.wait_ge(T.h, w)
            v.tensor_copy(mu_b[:, :], pC[2][:, :])
            v.tensor_copy(rs_b[:, :], pC[3][:, :])
            v.drain().then_inc(V.h, 1)
        bcv = V.inc()

        for dc in range(16):
            @em("v")
            def _(v, dc=dc, w=bcv, out_bf=out_bf):
                v.wait_ge(V.h, w)
                seg = r1[:, dc * 1024:(dc + 1) * 1024]
                v.tensor_tensor(seg, seg, mu_b[:, :], ALU.subtract)
                v.drain()
                if out_bf is not None:
                    v.tensor_tensor(out_bf[:, dc * 1024:(dc + 1) * 1024], seg, rs_b[:, :], ALU.mult)
                else:
                    v.tensor_tensor(seg, seg, rs_b[:, :], ALU.mult)
                v.drain().then_inc(V.h, 1)
            V.inc()
        return V.n

    ln1_done_v = layer_norm(r1_done, ln1bf, seg_waits=[r1_v[o] for o in range(16)])

    scope_f = ExitStack()
    w0t = scope_f.enter_context(nc.sbuf_tensor("w0t", [128, 16 * 1024], BF16))
    w1t = scope_f.enter_context(nc.sbuf_tensor("w1t", [128, 8 * 2048], BF16))
    dbuf = scope_f.enter_context(nc.sbuf_tensor("dbuf", [128, 8 * 1024], BF16))

    ffn1_t = {}
    gel_a = {}
    ffn2_t = {}
    eev_v = {}
    r2_v = {}
    for grp in range(8):
        @em("s")
        def _(s, grp=grp, wt=wt_done, wv=r1_done,
              wprev=(ffn1_t[(grp - 1, 7)] if grp > 0 else None)):
            s.wait_ge(WS[0].h, wt[0])
            s.wait_ge(WS[1].h, wt[1])
            s.wait_ge(V.h, wv)
            if wprev is not None:
                s.wait_ge(T.h, wprev)
            s.dma_start(out=w0t[:, :].rearrange("p (dc n) -> p dc n", dc=16),
                        in_=w0t_dram.ap()[:, grp * 1024:(grp + 1) * 1024]
                        .rearrange("(dc p) n -> p dc n", p=128)).then_inc(SW0.h, 16)
        w0_sl = SW0.inc(16)

        @em("s")
        def _(s, grp=grp, wt=wt_done, wv=r1_done,
              wprev=(ffn2_t[(grp - 1, 15)] if grp > 0 else None)):
            s.wait_ge(WS[0].h, wt[0])
            s.wait_ge(WS[1].h, wt[1])
            s.wait_ge(V.h, wv)
            if wprev is not None:
                s.wait_ge(T.h, wprev)
            s.dma_start(out=w1t[:, :].rearrange("p (j h) -> p j h", j=8),
                        in_=w1t_dram.ap()[grp * 1024:(grp + 1) * 1024, :]
                        .rearrange("(j p) h -> p j h", p=128)).then_inc(SW1.h, 16)
        w1_sl = SW1.inc(16)

        for j in range(8):
            ps = pC[j % 2]
            if (grp, j - 2) in gel_a:
                wg = gel_a[(grp, j - 2)]
            elif grp > 0:
                wg = gel_a[(grp - 1, j + 6)]
            else:
                wg = None

            @em("t")
            def _(t, grp=grp, j=j, ps=ps, wl=w0_sl, wv=ln1_done_v, wg=wg):
                t.wait_ge(SW0.h, wl)
                t.wait_ge(V.h, wv)
                if wg is not None:
                    t.wait_ge(A.h, wg)
                for ck in range(2):
                    for dc in range(16):
                        ins = t.matmul(ps[:, ck * 512:(ck + 1) * 512],
                                       w0t[:, dc * 1024 + j * 128:dc * 1024 + j * 128 + 128],
                                       ln1bf[:, dc * 1024 + ck * 512:dc * 1024 + (ck + 1) * 512],
                                       start=(dc == 0), stop=(dc == 15))
                ins.then_inc(T.h, 1)
            ffn1_t[(grp, j)] = T.inc()

            @em("a")
            def _(a, grp=grp, j=j, ps=ps, w=ffn1_t[(grp, j)], nt=grp * 8 + j, wb=bias_v):
                a.wait_ge(T.h, w)
                a.wait_ge(V.h, wb)
                ins = a.activation(dbuf[:, j * 1024:(j + 1) * 1024], ps[:, :], AF.Gelu,
                                   bias=b0_sb[:, nt:nt + 1], scale=1.0)
                a.drain().then_inc(A.h, 1)
                del ins
            gel_a[(grp, j)] = A.inc()

        for ht in range(16):
            ps = pC[2 + ht % 2]
            if ht >= 2:
                we = eev_v[(grp, ht - 2)]
            elif grp > 0:
                we = eev_v[(grp - 1, 14 + ht)]
            else:
                we = None

            @em("t")
            def _(t, grp=grp, ht=ht, ps=ps, wg=gel_a[(grp, 7)], wl=w1_sl, we=we):
                t.wait_ge(A.h, wg)
                t.wait_ge(SW1.h, wl)
                if we is not None:
                    t.wait_ge(V.h, we)
                for ck in range(2):
                    for j in range(8):
                        ins = t.matmul(ps[:, ck * 512:(ck + 1) * 512],
                                       w1t[:, j * 2048 + ht * 128:j * 2048 + ht * 128 + 128],
                                       dbuf[:, j * 1024 + ck * 512:j * 1024 + (ck + 1) * 512],
                                       start=(j == 0), stop=(j == 7))
                ins.then_inc(T.h, 1)
            ffn2_t[(grp, ht)] = T.inc()

            @em("v")
            def _(v, grp=grp, ht=ht, ps=ps, w=ffn2_t[(grp, ht)], wsp=ln1_done_v):
                v.wait_ge(T.h, w)
                if grp == 0:
                    v.wait_ge(V.h, wsp)
                    v.tensor_copy(r1[:, ht * 1024:(ht + 1) * 1024], ps[:, :])
                else:
                    v.tensor_tensor(r1[:, ht * 1024:(ht + 1) * 1024],
                                    r1[:, ht * 1024:(ht + 1) * 1024], ps[:, :], ALU.add)
                v.drain().then_inc(V.h, 1)
            eev_v[(grp, ht)] = V.inc()

            if grp == 7:
                @em("v")
                def _(v, ht=ht):
                    seg = r1[:, ht * 1024:(ht + 1) * 1024]
                    v.scalar_tensor_tensor(seg, seg, b1_sb[:, ht:ht + 1],
                                           ln1bf[:, ht * 1024:(ht + 1) * 1024], ALU.add, ALU.add)
                    v.drain().then_inc(V.h, 1)
                r2_v[ht] = V.inc()

    scope_f.close()
    r2_done = V.n

    ln2_done_v = layer_norm(r2_done, None,
                            seg_waits=[max(r2_v[h], eev_v[(7, 15)]) for h in range(16)])

    scope_o = ExitStack()
    outT = [scope_o.enter_context(nc.sbuf_tensor(f"outT{i}", [128, 2048], F32)) for i in range(2)]

    otr_v = {}
    st_out = {}
    for tt in range(8):
        @em("t")
        def _(t, tt=tt, w=ln2_done_v, wprev=(otr_v[tt - 1] if tt > 0 else None)):
            t.wait_ge(V.h, w)
            if wprev is not None:
                t.wait_ge(V.h, wprev)
            for dc in range(16):
                ps = pC[dc // 8]
                ins = t.transpose(ps[:, (dc % 8) * 128:(dc % 8) * 128 + 128],
                                  r1[:, dc * 1024 + tt * 128:dc * 1024 + tt * 128 + 128],
                                  id_f[:, :])
            ins.then_inc(T.h, 1)
        ot = T.inc()

        @em("v")
        def _(v, tt=tt, w=ot, wst=(st_out[tt - 2] if tt > 1 else None)):
            v.wait_ge(T.h, w)
            if wst is not None:
                v.wait_ge(SO[tt % 2].h, wst)
            v.tensor_copy(outT[tt % 2][:, 0:1024], pC[0][:, :])
            v.tensor_copy(outT[tt % 2][:, 1024:2048], pC[1][:, :])
            v.drain().then_inc(V.h, 1)
        otr_v[tt] = V.inc()

        @em("s")
        def _(s, tt=tt, w=otr_v[tt]):
            s.wait_ge(V.h, w)
            s.dma_start(out=out_own.ap()[tt * 128:(tt + 1) * 128, :],
                        in_=outT[tt % 2][:, :]).then_inc(SO[tt % 2].h, 16)
        st_out[tt] = SO[tt % 2].inc(16)

    @em("s")
    def _(s, w6=st_out[6], w7=st_out[7]):
        s.wait_ge(SO[0].h, w6)
        s.wait_ge(SO[1].h, w7)

    @em("g")
    def _(g, w6=st_out[6], w7=st_out[7]):
        g.wait_ge(SO[0].h, w6)
        g.wait_ge(SO[1].h, w7)

    block = ctx.enter_context(nc.Block())

    @block.gpsimd
    def _(eng):
        for e, fn in prog:
            if e == "g":
                fn(eng)

    @block.sync
    def _(eng):
        for e, fn in prog:
            if e == "s":
                fn(eng)

    @block.tensor
    def _(eng):
        for e, fn in prog:
            if e == "t":
                fn(eng)

    @block.scalar
    def _(eng):
        for e, fn in prog:
            if e == "a":
                fn(eng)

    @block.vector
    def _(eng):
        for e, fn in prog:
            if e == "v":
                fn(eng)

    scope_o.close()
    scope_ln.close()
    scope_c.close()
    ctx.close()
    return nc


_NC_CACHE = None


def _get_nc():
    global _NC_CACHE
    if _NC_CACHE is None:
        _NC_CACHE = build_nc()
    return _NC_CACHE


def kernel(L0_attn_x, L0_attn_w_qkv, L0_attn_w_o, L0_ffn_W0, L0_ffn_bias0,
           L0_ffn_W1, L0_ffn_bias1):
    x = np.ascontiguousarray(np.asarray(L0_attn_x, dtype=np.float32).reshape(S, DM))
    wqkv = np.asarray(L0_attn_w_qkv, dtype=np.float32)
    wo = np.ascontiguousarray(np.asarray(L0_attn_w_o, dtype=np.float32))
    w0 = np.asarray(L0_ffn_W0, dtype=np.float32)
    b0 = np.ascontiguousarray(np.asarray(L0_ffn_bias0, dtype=np.float32))
    w1 = np.asarray(L0_ffn_W1, dtype=np.float32)
    b1 = np.ascontiguousarray(np.asarray(L0_ffn_bias1, dtype=np.float32))

    in_maps = []
    for c in range(NC):
        cols = []
        for h in (2 * c, 2 * c + 1):
            for wi in range(3):
                cols.append(wqkv[:, wi * DM + h * DH:wi * DM + (h + 1) * DH])
        in_maps.append({
            "x_own": np.ascontiguousarray(x[c * TOK:(c + 1) * TOK]),
            "wqkv_own": np.ascontiguousarray(np.concatenate(cols, axis=1)),
            "wo_in": wo,
            "w0_own": np.ascontiguousarray(w0[c * (NF // NC):(c + 1) * (NF // NC)]),
            "b0_in": b0,
            "w1_own": np.ascontiguousarray(w1[c * (DM // NC):(c + 1) * (DM // NC)]),
            "b1_in": b1,
        })

    nc = _get_nc()
    res = run_bass_kernel_spmd(nc, in_maps, core_ids=list(range(NC)))
    out = np.concatenate([np.asarray(r["out_own"], dtype=np.float32) for r in res.results], axis=0)
    return out.reshape(1, S, DM)


# version 32 (final)
# speedup vs baseline: 1.0099x; 1.0099x over previous
import math
from contextlib import ExitStack

import numpy as np
import ml_dtypes

import concourse.bass as bass
import concourse.mybir as mybir
from concourse.bass_utils import run_bass_kernel_spmd
from concourse.aot_env import allow_ici_all_to_all

F32 = mybir.dt.float32
BF16 = mybir.dt.bfloat16
AF = mybir.ActivationFunctionType
ALU = mybir.AluOpType
AX = mybir.AxisListType

NC = 8
S = 8192
DM = 2048
NH = 16
DH = 128
NF = 8192
TOK = S // NC
SCALE = 1.0 / math.sqrt(DH)
EPS = 1e-5


def build_nc():
    nc = bass.Bass("TRN2", target_bir_lowering=False)

    x_own = nc.dram_tensor("x_own", [TOK, DM], F32, kind="ExternalInput")
    wqkv_own = nc.dram_tensor("wqkv_own", [DM, 6 * DH], F32, kind="ExternalInput")
    wo_in = nc.dram_tensor("wo_in", [DM, DM], F32, kind="ExternalInput")
    w0_own = nc.dram_tensor("w0_own", [NF // NC, DM], F32, kind="ExternalInput")
    b0_in = nc.dram_tensor("b0_in", [NF], F32, kind="ExternalInput")
    w1_own = nc.dram_tensor("w1_own", [DM // NC, NF], F32, kind="ExternalInput")
    b1_in = nc.dram_tensor("b1_in", [DM], F32, kind="ExternalInput")
    out_own = nc.dram_tensor("out_own", [TOK, DM], F32, kind="ExternalOutput")

    xbf_own = nc.dram_tensor("xbf_own", [TOK, DM], BF16)
    xbf_all = nc.dram_tensor("xbf_all", [S, DM], BF16, addr_space="Shared")
    w0bf_own = nc.dram_tensor("w0bf_own", [NF // NC, DM], BF16)
    w0bf = nc.dram_tensor("w0bf", [NF, DM], BF16, addr_space="Shared")
    w1bf_own = nc.dram_tensor("w1bf_own", [DM // NC, NF], BF16)
    w1bf = nc.dram_tensor("w1bf", [DM, NF], BF16, addr_space="Shared")
    b0bf = nc.dram_tensor("b0bf", [64, 128], BF16)
    b1bf = nc.dram_tensor("b1bf", [16, 128], BF16)
    vals_own = nc.dram_tensor("vals_own", [NC * 2 * DH, TOK], BF16)
    vals_a2a = nc.dram_tensor("vals_a2a", [NC * 2 * DH, TOK], BF16)
    w0t_dram = nc.dram_tensor("w0t_dram", [DM, NF], BF16)
    w1t_dram = nc.dram_tensor("w1t_dram", [NF, DM], BF16)
    wobf = nc.dram_tensor("wobf", [DM, DM], BF16)

    ones_col_bf = nc.inline_tensor(np.ones((128, 1), dtype=ml_dtypes.bfloat16), name="c_ocb")
    ones_row_bf = nc.inline_tensor(np.ones((1, 128), dtype=ml_dtypes.bfloat16), name="c_orb")
    ones_col_f = nc.inline_tensor(np.ones((128, 1), dtype=np.float32), name="c_ocf")
    ones_row_f = nc.inline_tensor(np.ones((1, 128), dtype=np.float32), name="c_orf")
    ident_f = nc.inline_tensor(np.eye(128, dtype=np.float32), name="c_idf")
    ident_bf = nc.inline_tensor(np.eye(128, dtype=ml_dtypes.bfloat16), name="c_idb")

    ctx = ExitStack()

    class Ctr:
        def __init__(self, name):
            self.h = ctx.enter_context(nc.semaphore(name))
            self.n = 0

        def inc(self, k=1):
            self.n += k
            return self.n

    GD = Ctr("GD")
    GX = Ctr("GX")
    G0 = Ctr("G0")
    G1 = Ctr("G1")
    GQ = Ctr("GQ")
    GW = Ctr("GW")
    SB = Ctr("SB")
    SXP = [Ctr("SXP0"), Ctr("SXP1")]
    WX = [Ctr("WX0"), Ctr("WX1")]
    WS = [Ctr("WS0"), Ctr("WS1")]
    XO = [Ctr("XO0"), Ctr("XO1")]
    SPJ = [Ctr("SPJ0"), Ctr("SPJ1")]
    SO = [Ctr("SO0"), Ctr("SO1")]
    SW0 = Ctr("SW0")
    SW1 = Ctr("SW1")
    CCX = Ctr("CCX")
    CC0 = Ctr("CC0")
    CC1 = Ctr("CC1")
    CCV = Ctr("CCV")
    CC = Ctr("CC")
    SL = Ctr("SL")
    SX = Ctr("SX")
    SY = Ctr("SY")
    SS = Ctr("SS")
    T = Ctr("T")
    A = Ctr("A")
    V = Ctr("V")

    prog = []

    def em(eng):
        def dec(fn):
            prog.append((eng, fn))
        return dec

    b0_sb = ctx.enter_context(nc.sbuf_tensor("b0_sb", [128, 64], F32))
    b1_sb = ctx.enter_context(nc.sbuf_tensor("b1_sb", [128, 16], F32))
    b0t = ctx.enter_context(nc.sbuf_tensor("b0t", [128, 64], BF16))
    b1t = ctx.enter_context(nc.sbuf_tensor("b1t", [128, 16], BF16))
    oc_bf = ctx.enter_context(nc.sbuf_tensor("oc_bf", [128, 1], BF16))
    or_bf = ctx.enter_context(nc.sbuf_tensor("or_bf", [1, 128], BF16))
    oc_f = ctx.enter_context(nc.sbuf_tensor("oc_f", [128, 1], F32))
    or_f = ctx.enter_context(nc.sbuf_tensor("or_f", [1, 128], F32))
    id_f = ctx.enter_context(nc.sbuf_tensor("id_f", [128, 128], F32))
    id_bf = ctx.enter_context(nc.sbuf_tensor("id_bf", [128, 128], BF16))

    @em("g")
    def _(g):
        g.dma_start(out=xbf_own.ap(), in_=x_own.ap()).then_inc(GX.h, 16)
        g.dma_start(out=b0bf.ap(), in_=b0_in.ap().rearrange("(a b) -> a b", b=128)).then_inc(GD.h, 16)
        g.dma_start(out=b1bf.ap(), in_=b1_in.ap().rearrange("(a b) -> a b", b=128)).then_inc(GD.h, 16)
    GX.inc(16)
    casts_gd = GD.inc(32)

    def _emit_w_casts():
        @em("g")
        def _(g):
            g.wait_ge(SXP[1].h, 512)
            g.dma_start(out=w0bf_own.ap(), in_=w0_own.ap()).then_inc(G0.h, 16)
            g.dma_start(out=w1bf_own.ap(), in_=w1_own.ap()).then_inc(G1.h, 16)
            g.dma_start(out=wobf.ap(), in_=wo_in.ap()).then_inc(GD.h, 16)
        G0.inc(16); G1.inc(16)
        return GD.inc(16)

    @em("s")
    def _(s, w=casts_gd):
        s.wait_ge(GD.h, w)
        s.dma_start(out=b0t[:, :], in_=b0bf.ap(), transpose=True).then_inc(SB.h, 16)
        s.dma_start(out=b1t[:, :], in_=b1bf.ap(), transpose=True).then_inc(SB.h, 16)
    bias_sl = SB.inc(32)

    @em("v")
    def _(v, w=bias_sl):
        v.wait_ge(SB.h, w)
        v.tensor_copy(b0_sb[:, :], b0t[:, :])
        v.tensor_copy(b1_sb[:, :], b1t[:, :])
        v.drain().then_inc(V.h, 1)
    bias_v = V.inc()

    def _emit_ag_x():
        @em("g")
        def _(g):
            g.wait_ge(GX.h, 16)
            g.collective_compute("AllGather", ALU.bypass, replica_groups=[list(range(NC))],
                                 ins=[xbf_own.ap().opt()],
                                 outs=[xbf_all.ap().opt()]).then_inc(CCX.h, 1)
        return CCX.inc()

    def _emit_ag_w0(wait_loads=None):
        @em("g")
        def _(g, wait_loads=wait_loads):
            g.wait_ge(G0.h, 16)
            if wait_loads is not None:
                g.wait_ge(wait_loads[0].h, wait_loads[1])
            g.collective_compute("AllGather", ALU.bypass, replica_groups=[list(range(NC))],
                                 ins=[w0bf_own.ap().opt()], outs=[w0bf.ap().opt()]).then_inc(CC0.h, 1)
        return CC0.inc()

    def _emit_ag_w1():
        @em("g")
        def _(g):
            g.wait_ge(G1.h, 16)
            g.collective_compute("AllGather", ALU.bypass, replica_groups=[list(range(NC))],
                                 ins=[w1bf_own.ap().opt()], outs=[w1bf.ap().opt()]).then_inc(CC1.h, 1)
        return CC1.inc()

    @em("s")
    def _(s):
        s.dma_start(out=oc_bf[:, :], in_=ones_col_bf.ap()).then_inc(SL.h, 16)
        s.dma_start(out=or_bf[:, :], in_=ones_row_bf.ap()).then_inc(SL.h, 16)
        s.dma_start(out=oc_f[:, :], in_=ones_col_f.ap()).then_inc(SL.h, 16)
        s.dma_start(out=or_f[:, :], in_=ones_row_f.ap()).then_inc(SL.h, 16)
        s.dma_start(out=id_f[:, :], in_=ident_f.ap()).then_inc(SL.h, 16)
        s.dma_start(out=id_bf[:, :], in_=ident_bf.ap()).then_inc(SL.h, 16)
    consts_sl = SL.inc(16 * 6)

    scope_b = ExitStack()
    qkv_sb = scope_b.enter_context(nc.sbuf_tensor("qkv_sb", [128, 6 * S], BF16))
    psA = scope_b.enter_context(nc.psum_tensor("psA", [128, 1536], F32))
    psB = scope_b.enter_context(nc.psum_tensor("psB", [128, 2048], F32))
    c_ps = scope_b.enter_context(nc.psum_tensor("c_ps", [128, 320], F32))

    def qt(h, wi):
        o = (h * 3 + wi) * S
        return qkv_sb[:, o:o + S]

    scope_b1 = ExitStack()
    wq_sb = scope_b1.enter_context(nc.sbuf_tensor("wq_sb", [128, 6 * 16 * 128], BF16))
    sqb = scope_b1.enter_context(nc.sbuf_tensor("sqb", [128, 2048], BF16))
    NSLOT = 20
    xF = scope_b1.enter_context(nc.sbuf_tensor("xF", [128, NSLOT * 2048], BF16))

    def slot_of(ts2, dc):
        return (ts2 * 16 + dc) % NSLOT

    @em("g")
    def _(g):
        g.dma_start(out=wq_sb[:, :].rearrange("p (ct dc col) -> p ct dc col", ct=6, dc=16),
                    in_=wqkv_own.ap().rearrange("(dc p) (ct col) -> p ct dc col", p=128, col=128)
                    ).then_inc(GQ.h, 16)
    wq_gd = GQ.inc(16)

    cc_x = _emit_ag_x()
    wobf_gd = _emit_w_casts()
    cc_w0 = _emit_ag_w0(wait_loads=(SXP[1], 512))
    cc_w1 = _emit_ag_w1()

    def wqt(h, wi, dc):
        o = ((h * 3 + wi) * 16 + dc) * 128
        return wq_sb[:, o:o + 128]

    series_T = []
    series_V = []
    rel_t = {}
    load_sl = {}
    SXK = SXP

    NORM_BASE = {0: 64, 1: 128, 3: 192, 4: 256}
    norm_pending = []
    norm_entry = 0
    norm_mm_t = []
    series_serial = 0

    def flush_norms(upto_serial):
        while norm_pending and norm_pending[0][0] <= upto_serial:
            _, sq_v, colbase, tsv, sh = norm_pending.pop(0)

            @em("t")
            def _(t, sq_v=sq_v, colbase=colbase, tsv=tsv, sh=sh):
                t.wait_ge(V.h, sq_v)
                for j in range(8):
                    col = colbase + tsv * 8 + j
                    ins = t.matmul(c_ps[:, col:col + 1],
                                   sqb[:, sh * 1024 + j * 128:sh * 1024 + (j + 1) * 128],
                                   oc_bf[:, :], start=True, stop=True)
                ins.then_inc(T.h, 1)
            norm_mm_t.append(T.inc())

    for ts2 in range(4):
        sem_x = SXK[ts2 % 2]
        for dc in range(16):
            slot = slot_of(ts2, dc)
            prev_use = None
            for back in (1, 2):
                tp = ts2 - back
                if tp < 0:
                    break
                off = slot - (tp * 16) % NSLOT
                dcp = off % NSLOT
                if 0 <= dcp < 16:
                    prev_use = (tp, dcp)
                    break

            @em("s")
            def _(s, ts2=ts2, dc=dc, slot=slot, sem_x=sem_x, wprev=sem_x.n,
                  wrel=(rel_t.get(prev_use) if prev_use else None)):
                s.wait_ge(CCX.h, 1)
                if wrel is not None:
                    s.wait_ge(T.h, wrel)
                if dc >= 1:
                    s.wait_ge(sem_x.h, wprev)
                s.dma_start(out=xF[:, slot * 2048:(slot + 1) * 2048],
                            in_=xbf_all.ap()[ts2 * 2048:(ts2 + 1) * 2048, dc * 128:(dc + 1) * 128],
                            transpose=True).then_inc(sem_x.h, 16)
            sem_x.inc(16)
        load_sl[ts2] = sem_x.n

        for half in range(2):
            ts = ts2 * 2 + half
            release = (half == 1)
            for ci in range(6):
                flush_norms(series_serial - 2)
                series_serial += 1
                hh, wi = divmod(ci, 3)
                sidx = ts * 6 + ci
                ps = (psA, psB)[ci % 2]
                w_evac = series_V[sidx - 2] if sidx >= 2 else None
                rel_this = release and ci == 5 and ts2 < 3
                if rel_this:
                    for dc in range(16):
                        rel_t[(ts2, dc)] = T.n + dc + 1
                    T.n += 16
                else:
                    T.n += 1

                first_series = (half == 0 and ci == 0)

                @em("t")
                def _(t, hh=hh, wi=wi, ts=ts, ts2=ts2, half=half, ps=ps, ci=ci,
                      sem_x=sem_x, xf_sl=load_sl[ts2], w_evac=w_evac, wq_gd=wq_gd,
                      wc=consts_sl, rel_this=rel_this, first_series=first_series):
                    t.wait_ge(SL.h, wc)
                    if not first_series:
                        t.wait_ge(sem_x.h, xf_sl)
                    t.wait_ge(GQ.h, wq_gd)
                    if w_evac is not None:
                        t.wait_ge(V.h, w_evac)
                    for ck in range(2):
                        for dc in range(16):
                            if first_series and ck == 0:
                                t.wait_ge(sem_x.h, xf_sl - 256 + 16 * (dc + 1))
                            base = slot_of(ts2, dc) * 2048 + half * 1024 + ck * 512
                            ins = t.matmul(ps[:, ck * 512:(ck + 1) * 512], wqt(hh, wi, dc),
                                           xF[:, base:base + 512],
                                           start=(dc == 0), stop=(dc == 15))
                            if rel_this and ck == 1:
                                ins.then_inc(T.h, 1)
                    if not rel_this:
                        ins.then_inc(T.h, 1)
                series_T.append(T.n)

                @em("v")
                def _(v, hh=hh, wi=wi, ts=ts, ps=ps, w=series_T[-1]):
                    v.wait_ge(T.h, w)
                    v.tensor_copy(qt(hh, wi)[:, ts * 1024:(ts + 1) * 1024], ps[:, 0:1024])
                    v.drain().then_inc(V.h, 1)
                series_V.append(V.inc())

                if ci in NORM_BASE:
                    sq_half = norm_entry % 2
                    wmm = norm_mm_t[norm_entry - 2] if norm_entry >= 2 else None

                    @em("v")
                    def _(v, hh=hh, wi=wi, ts=ts, sh=sq_half, wmm=wmm):
                        if wmm is not None:
                            v.wait_ge(T.h, wmm)
                        src = qt(hh, wi)[:, ts * 1024:(ts + 1) * 1024]
                        v.tensor_tensor(sqb[:, sh * 1024:(sh + 1) * 1024], src, src, ALU.mult)
                        v.drain().then_inc(V.h, 1)
                    norm_pending.append((series_serial, V.inc(), NORM_BASE[ci], ts, sq_half))
                    norm_entry += 1

    flush_norms(series_serial)
    norms_done_t = T.n
    qkv_done_v = V.n
    qkv_done_t = T.n


    scope_b1.close()
    scope_b2 = ExitStack()
    p_sb = [scope_b2.enter_context(nc.sbuf_tensor(f"p_sb{i}", [128, S], BF16)) for i in range(3)]
    vals_sb = scope_b2.enter_context(nc.sbuf_tensor("vals_sb", [128, S], BF16))
    cRow = scope_b2.enter_context(nc.sbuf_tensor("cRow", [1, S // 128 * 128], BF16))
    negB = scope_b2.enter_context(nc.sbuf_tensor("negB", [128, 128], F32))
    zparts = scope_b2.enter_context(nc.sbuf_tensor("zparts", [128, 24], F32))
    z1 = scope_b2.enter_context(nc.sbuf_tensor("z1", [128, 1], F32))
    zr = scope_b2.enter_context(nc.sbuf_tensor("zr", [128, 1], F32))
    zrb = [scope_b2.enter_context(nc.sbuf_tensor(f"zrb{i}", [128, 1], BF16)) for i in range(2)]
    zpad = scope_b2.enter_context(nc.sbuf_tensor("zpad", [128, 1], F32))
    wtb = [scope_b2.enter_context(nc.sbuf_tensor(f"wtb{i}", [128, 2048], BF16)) for i in range(2)]
    c_sb = scope_b2.enter_context(nc.sbuf_tensor("c_sb", [128, 64], F32))
    nrmH = scope_b2.enter_context(nc.sbuf_tensor("nrmH", [128, 256], F32))
    mk1 = scope_b2.enter_context(nc.sbuf_tensor("mk1", [128, 1], F32))
    mk2 = scope_b2.enter_context(nc.sbuf_tensor("mk2", [1, 128], F32))
    mks = scope_b2.enter_context(nc.sbuf_tensor("mks", [1, 1], F32))
    mksb = scope_b2.enter_context(nc.sbuf_tensor("mksb", [1, 1], BF16))
    scl = scope_b2.enter_context(nc.sbuf_tensor("scl", [128, 2], F32))

    wt_ss = []
    for dc in range(16):
        for ns in range(4):
            @em("s")
            def _(s, dc=dc, ns=ns, wc=cc_w0, i=len(wt_ss), wq=qkv_done_t):
                s.wait_ge(CC0.h, wc)
                s.wait_ge(T.h, wq)
                if i >= 2:
                    s.wait_ge(WS[i % 2].h, wt_ss[i - 2])
                s.dma_start(out=wtb[i % 2][:, :],
                            in_=w0bf.ap()[ns * 2048:(ns + 1) * 2048, dc * 128:(dc + 1) * 128],
                            transpose=True).then_inc(WX[i % 2].h, 16)
            cur_sx = WX[len(wt_ss) % 2].inc(16)

            @em("s")
            def _(s, dc=dc, ns=ns, w=cur_sx, i=len(wt_ss)):
                s.wait_ge(WX[i % 2].h, w)
                s.dma_start(out=w0t_dram.ap()[dc * 128:(dc + 1) * 128, ns * 2048:(ns + 1) * 2048],
                            in_=wtb[i % 2][:, :]).then_inc(WS[i % 2].h, 16)
            wt_ss.append(WS[len(wt_ss) % 2].inc(16))
    for ntc in range(64):
        @em("s")
        def _(s, ntc=ntc, wc=cc_w1, i=len(wt_ss)):
            s.wait_ge(CC1.h, wc)
            s.wait_ge(WS[i % 2].h, wt_ss[i - 2])
            s.dma_start(out=wtb[i % 2][:, :],
                        in_=w1bf.ap()[:, ntc * 128:(ntc + 1) * 128],
                        transpose=True).then_inc(WX[i % 2].h, 16)
        cur_sx = WX[len(wt_ss) % 2].inc(16)

        @em("s")
        def _(s, ntc=ntc, w=cur_sx, i=len(wt_ss)):
            s.wait_ge(WX[i % 2].h, w)
            s.dma_start(out=w1t_dram.ap()[ntc * 128:(ntc + 1) * 128, :],
                        in_=wtb[i % 2][:, :]).then_inc(WS[i % 2].h, 16)
        wt_ss.append(WS[len(wt_ss) % 2].inc(16))
    wt_done = (WS[0].n, WS[1].n)

    vals_store_ss = []

    @em("v")
    def _(v, w=norms_done_t):
        v.wait_ge(T.h, w)
        v.tensor_copy(nrmH[:, :], c_ps[:, 64:320])
        v.drain().then_inc(V.h, 1)
    nev = V.inc()

    ready_vh = []
    prev_tr_t = None
    for ph in range(2):
        @em("v")
        def _(v, ph=ph, wtr=prev_tr_t):
            if wtr is not None:
                v.wait_ge(T.h, wtr)
            v.reduce_max(mk1[:, :], nrmH[:, ph * 128 + 64:ph * 128 + 128], axis=AX.X)
            v.drain().then_inc(V.h, 1)
        mkv = V.inc()

        @em("t")
        def _(t, w=mkv, wc=consts_sl):
            t.wait_ge(V.h, w)
            t.wait_ge(SL.h, wc)
            t.transpose(psB[0:1, 0:128], mk1[:, :], id_f[:, :]).then_inc(T.h, 1)
        trt = T.inc()
        prev_tr_t = trt

        @em("v")
        def _(v, w=trt):
            v.wait_ge(T.h, w)
            v.tensor_copy(mk2[:, :], psB[0:1, 0:128])
            v.drain()
            v.reduce_max(mks[:, :], mk2[:, :], axis=AX.X)
            v.drain()
            v.tensor_scalar(mks[:, :], mks[:, :], float(1.0404 / DH), None, ALU.mult)
            v.drain()
            v.tensor_copy(mksb[:, :], mks[:, :])
            v.drain().then_inc(V.h, 1)
        mksv = V.inc()

        @em("t")
        def _(t, w=mksv):
            t.wait_ge(V.h, w)
            t.matmul(psB[:, 128:129], or_bf[:, :], mksb[:, :], start=True, stop=True).then_inc(T.h, 1)
        bct = T.inc()

        @em("v")
        def _(v, ph=ph, w=bct):
            v.wait_ge(T.h, w)
            v.tensor_copy(scl[:, ph:ph + 1], psB[:, 128:129])
            v.drain().then_inc(V.h, 1)
        sclv = V.inc()

        @em("a")
        def _(a, ph=ph, w=sclv):
            a.wait_ge(V.h, w)
            a.activation(negB[:, ph * 64:(ph + 1) * 64], nrmH[:, ph * 128:ph * 128 + 64],
                         AF.Sqrt, bias=0.0, scale=scl[:, ph:ph + 1])
            a.drain().then_inc(A.h, 1)
        sqa = A.inc()

        @em("v")
        def _(v, ph=ph, w=sqa):
            v.wait_ge(A.h, w)
            v.tensor_scalar(negB[:, ph * 64:(ph + 1) * 64], negB[:, ph * 64:(ph + 1) * 64],
                            -1.0, None, ALU.mult)
            v.drain().then_inc(V.h, 1)
        ready_vh.append(V.inc())

    def head_b(h, ready_v, prev):
        PASS_EVEN = [(0, 1536), (1, 2048), (0, 1536), (1, 2048), (0, 1024)]
        PASS_ODD = [(1, 2048), (0, 1536), (1, 2048), (0, 1536), (1, 1024)]

        def passes_of(qb):
            plist = PASS_EVEN if qb % 2 == 0 else PASS_ODD
            out = []
            k0 = 0
            for tile, klen in plist:
                out.append((tile, k0, klen))
                k0 += klen
            return out

        def tile_passes(qb, tile):
            return [i for i, (t, _, _) in enumerate(passes_of(qb)) if t == tile]

        last_mm = {}
        exp_a = {}
        zr_v = {}
        col_t = {}

        PREV = {0: (-1, 3), 1: (-1, 4), 2: (0, 0), 3: (0, 1), 4: (0, 2)}

        def emit_scores(qb, p):
            pl = passes_of(qb)
            tile, k0, klen = pl[p]
            ps = (psA, psB)[tile]
            off = 0
            dq, dp = PREV[p]
            prevp = (qb + dq, dp)
            waits = [("V", ready_v)]
            if prevp in exp_a:
                waits.append(("A", exp_a[prevp]))
            if qb == 0 and p == 0 and prev is not None:
                waits.append(("V", prev["done"]))

            @em("t")
            def _(t, h=h, qb=qb, ps=ps, k0=k0, klen=klen, off=off, waits=tuple(waits)):
                for kind, w in waits:
                    t.wait_ge({"V": V.h, "A": A.h}[kind], w)
                for j in range((klen + 511) // 512):
                    w0 = j * 512
                    w1 = min(klen, w0 + 512)
                    ins = t.matmul(ps[:, off + w0:off + w1],
                                   qt(h, 0)[:, qb * 128:(qb + 1) * 128],
                                   qt(h, 1)[:, k0 + w0:k0 + w1],
                                   start=True, stop=True)
                ins.then_inc(T.h, 1)
            last_mm[(qb, p)] = T.inc()

            pw = []
            if p == 0 and (qb - 4) in zr_v:
                pw.append(("V", zr_v[qb - 4]))
            if p == 0 and (qb - 3) in col_t:
                pw.append(("T", col_t[qb - 3]))
            if qb == 0 and p == 0 and prev is not None:
                pw.append(("V", prev["zr63"]))
                pw.append(("T", prev["col63"]))

            @em("a")
            def _(a, h=h, qb=qb, p=p, ps=ps, k0=k0, klen=klen, off=off,
                  w=last_mm[(qb, p)], pw=tuple(pw)):
                a.wait_ge(T.h, w)
                for kind, wv in pw:
                    a.wait_ge({"V": V.h, "T": T.h}[kind], wv)
                zo = (qb % 4) * 6 + p
                ins = a.activation(p_sb[qb % 3][:, k0:k0 + klen], ps[:, off:off + klen],
                                   AF.Exp, bias=negB[:, h * 64 + qb:h * 64 + qb + 1],
                                   scale=float(SCALE),
                                   accum_out=zparts[:, zo:zo + 1])
                if p == 4 and qb % 2 == 1:
                    a.drain().then_inc(A.h, 1)
                else:
                    ins.then_inc(A.h, 1)
            exp_a[(qb, p)] = A.inc()

        def emit_zr(qb, wa):
            if qb >= 2:
                wcol = col_t.get(qb - 2)
            else:
                wcol = prev[("col62", "col63")[qb]] if prev is not None else None

            @em("v")
            def _(v, qb=qb, w=wa, wcol=wcol):
                v.wait_ge(A.h, w)
                if wcol is not None:
                    v.wait_ge(T.h, wcol)
                v.reduce_sum(z1[:, :], zparts[:, (qb % 4) * 6:(qb % 4) * 6 + 5], axis=AX.X)
                v.drain()
                v.reciprocal(zr[:, :], z1[:, :])
                v.drain()
                v.tensor_copy(zrb[qb % 2][:, :], zr[:, :])
                v.drain().then_inc(V.h, 1)
            zr_v[qb] = V.inc()

        def emit_colsum_half(qb, hf):
            @em("t")
            def _(t, h=h, qb=qb, hf=hf, w=zr_v[qb],
                  wcev=(prev["cev"] if (qb == 0 and hf == 0 and prev is not None) else None)):
                t.wait_ge(V.h, w)
                if wcev is not None:
                    t.wait_ge(V.h, wcev)
                for j in range(hf * 32, hf * 32 + 32):
                    ins = t.matmul(c_ps[:, j:j + 1], p_sb[qb % 3][:, j * 128:(j + 1) * 128],
                                   zrb[qb % 2][:, :],
                                   start=(qb == 0 and j == 0), stop=(qb == 63 and j == 63),
                                   skip_group_check=True)
                ins.then_inc(T.h, 1)
            return T.inc()

        def emit_colsum(qb, hf=None):
            if hf is None:
                emit_colsum_half(qb, 0)
                col_t[qb] = emit_colsum_half(qb, 1)
            else:
                r = emit_colsum_half(qb, hf)
                if hf == 1:
                    col_t[qb] = r

        for qb in range(64):
            emit_scores(qb, 0)
            emit_scores(qb, 1)
            if qb >= 2:
                emit_colsum(qb - 2, 0)
            emit_scores(qb, 2)
            emit_scores(qb, 3)
            if qb >= 2:
                emit_colsum(qb - 2, 1)
            emit_scores(qb, 4)
            if qb % 2 == 1:
                emit_zr(qb - 1, exp_a[(qb, 4)])
                emit_zr(qb, exp_a[(qb, 4)])
        emit_colsum(62)
        emit_colsum(63)

        @em("v")
        def _(v, w=col_t[63]):
            v.wait_ge(T.h, w)
            v.tensor_copy(c_sb[:, :], c_ps[:, 0:64])
            v.drain().then_inc(V.h, 1)
        cev_v = {63: V.inc()}

        cbf_v = cev_v[63]

        cev_list = []
        for grp4 in range(4):
            @em("t")
            def _(t, grp4=grp4, w=cbf_v, wprev=(cev_list[grp4 - 1] if grp4 >= 1 else None)):
                t.wait_ge(V.h, w)
                if wprev is not None:
                    t.wait_ge(V.h, wprev)
                for j in range(16):
                    col = grp4 * 16 + j
                    ins = t.transpose(psB[0:1, j * 128:(j + 1) * 128],
                                      c_sb[:, col:col + 1], id_f[:, :])
                ins.then_inc(T.h, 1)
            tcur = T.inc()

            @em("v")
            def _(v, grp4=grp4, w=tcur):
                v.wait_ge(T.h, w)
                v.tensor_copy(cRow[0:1, grp4 * 2048:(grp4 + 1) * 2048], psB[0:1, 0:2048])
                v.drain().then_inc(V.h, 1)
            cev_list.append(V.inc())
        cRow_v = V.n

        vm_last = []
        for grp in range(16):
            ps = (psA, psB)[grp % 2]

            @em("t")
            def _(t, grp=grp, ps=ps, w=cRow_v, wprev=(vm_last[grp - 2] if grp >= 2 else None)):
                t.wait_ge(V.h, w)
                if wprev is not None:
                    t.wait_ge(V.h, wprev)
                t.matmul(ps[:, 512:1024], or_bf[:, :],
                         cRow[0:1, grp * 512:(grp + 1) * 512], start=True, stop=True).then_inc(T.h, 1)
            bct2 = T.inc()

            wss = vals_store_ss[0] if (h == 1 and grp == 0) else None

            @em("v")
            def _(v, h=h, grp=grp, ps=ps, w=bct2, wss=wss):
                v.wait_ge(T.h, w)
                if wss is not None:
                    v.wait_ge(SS.h, wss)
                v.tensor_tensor(vals_sb[:, grp * 512:(grp + 1) * 512],
                                qt(h, 2)[:, grp * 512:(grp + 1) * 512], ps[:, 512:1024], ALU.mult)
                v.drain().then_inc(V.h, 1)
            vm_last.append(V.inc())

        @em("s")
        def _(s, h=h, w=vm_last[-1]):
            s.wait_ge(V.h, w)
            dst = vals_own.ap().rearrange("(sh r) t -> sh r t", sh=NC)
            dst = dst[:, h * 128:(h + 1) * 128, :].rearrange("sh p t -> p sh t")
            s.dma_start(out=dst, in_=vals_sb[:, :].rearrange("p (sh t) -> p sh t", sh=NC)
                        ).then_inc(SS.h, 16)
        vals_store_ss.append(SS.inc(16))
        return {"done": V.n, "col62": col_t[62], "col63": col_t[63],
                "zr63": zr_v[63], "cev": cev_v[63]}

    ctx0 = head_b(0, ready_vh[1], None)
    head_b(1, ready_vh[1], ctx0)

    scope_b2.close()
    scope_b.close()

    scope_c = ExitStack()
    r1 = scope_c.enter_context(nc.sbuf_tensor("r1", [128, 16 * 1024], F32))
    mu_b = scope_c.enter_context(nc.sbuf_tensor("mu_b", [128, 1024], F32))
    rs_b = scope_c.enter_context(nc.sbuf_tensor("rs_b", [128, 1024], F32))
    row_mu = scope_c.enter_context(nc.sbuf_tensor("row_mu", [1, 1024], F32))
    row_v = scope_c.enter_context(nc.sbuf_tensor("row_v", [1, 1024], F32))
    row_t = scope_c.enter_context(nc.sbuf_tensor("row_t", [1, 1024], F32))
    sq_buf = [scope_c.enter_context(nc.sbuf_tensor(f"sq{i}", [128, 1024], F32)) for i in range(2)]
    pC = [scope_c.enter_context(nc.psum_tensor(f"pC{i}", [128, 1024], F32)) for i in range(4)]

    scope_c1 = ExitStack()
    wo_sb = scope_c1.enter_context(nc.sbuf_tensor("wo_sb", [128, 16 * 2048], BF16))

    @em("g")
    def _(g, w=vals_store_ss[1], wt=wt_done, wc=wobf_gd):
        g.wait_ge(SS.h, w)
        g.wait_ge(GD.h, wc)
        g.wait_ge(WS[0].h, wt[0])
        g.wait_ge(WS[1].h, wt[1])
        g.dma_start(out=wo_sb[:, :].rearrange("p (dc o) -> p dc o", dc=16),
                    in_=wobf.ap().rearrange("(dc p) o -> p dc o", p=128)).then_inc(GW.h, 16)
    wo_gd = GW.inc(16)

    @em("g")
    def _(g, w=vals_store_ss[1]):
        g.wait_ge(SS.h, w)
        with allow_ici_all_to_all():
            g.collective_compute("AllToAll", ALU.bypass, replica_groups=[list(range(NC))],
                                 ins=[vals_own.ap().opt()], outs=[vals_a2a.ap().opt()]).then_inc(CCV.h, 1)
    cc_vals = CCV.inc()

    valsF = scope_c1.enter_context(nc.sbuf_tensor("valsF", [128, 16 * 1024], BF16))
    xoF = [scope_c1.enter_context(nc.sbuf_tensor(f"xoF{i}", [128, 1024], BF16)) for i in range(2)]

    @em("s")
    def _(s, w=cc_vals, wt=wt_done):
        s.wait_ge(CCV.h, w)
        s.wait_ge(WS[0].h, wt[0])
        s.wait_ge(WS[1].h, wt[1])
        s.dma_start(out=valsF[:, :].rearrange("p (hd t) -> p hd t", hd=16),
                    in_=vals_a2a.ap().rearrange("(hd p) t -> p hd t", p=128)).then_inc(SL.h, 16)
    valsF_sl = SL.inc(16)

    xo_sl = {}

    def emit_xo_load(dc, wait_v):
        @em("s")
        def _(s, dc=dc, wait_v=wait_v):
            s.wait_ge(GX.h, 16)
            if wait_v is not None:
                s.wait_ge(V.h, wait_v)
            s.dma_start(out=xoF[dc % 2][:, :], in_=xbf_own.ap()[:, dc * 128:(dc + 1) * 128],
                        transpose=True).then_inc(XO[dc % 2].h, 16)
        xo_sl[dc] = XO[dc % 2].inc(16)

    emit_xo_load(0, None)
    emit_xo_load(1, None)

    r1_v = {}
    for o in range(16):
        ps = pC[o % 2]

        @em("t")
        def _(t, o=o, ps=ps, wv=valsF_sl, wg=wo_gd, wx=xo_sl[o],
              wfree=(r1_v[o - 2] if o >= 2 else None)):
            t.wait_ge(SL.h, wv)
            t.wait_ge(GW.h, wg)
            t.wait_ge(XO[o % 2].h, wx)
            if wfree is not None:
                t.wait_ge(V.h, wfree)
            for ck in range(2):
                for hd in range(16):
                    ins = t.matmul(ps[:, ck * 512:(ck + 1) * 512],
                                   wo_sb[:, (hd * 16 + o) * 128:(hd * 16 + o) * 128 + 128],
                                   valsF[:, hd * 1024 + ck * 512:hd * 1024 + (ck + 1) * 512],
                                   start=(hd == 0), stop=(hd == 15))
            ins.then_inc(T.h, 1)
        opt = T.inc()

        @em("v")
        def _(v, o=o, ps=ps, w=opt):
            v.wait_ge(T.h, w)
            v.scalar_tensor_tensor(r1[:, o * 1024:(o + 1) * 1024], ps[:, :], 1.0,
                                   xoF[o % 2][:, :], ALU.mult, ALU.add)
            v.drain().then_inc(V.h, 1)
        r1_v[o] = V.inc()
        if o + 2 < 16:
            emit_xo_load(o + 2, r1_v[o])
    r1_done = V.n

    scope_c1.close()
    scope_ln = ExitStack()
    ln1bf = scope_ln.enter_context(nc.sbuf_tensor("ln1bf", [128, 16 * 1024], BF16))

    def layer_norm(wait_v, out_bf, seg_waits=None):
        def segw(dc):
            return seg_waits[dc] if seg_waits is not None else wait_v

        @em("t")
        def _(t, sw=tuple(segw(dc) for dc in range(16))):
            for ck in range(2):
                for dc in range(16):
                    if ck == 0:
                        t.wait_ge(V.h, sw[dc])
                    ins = t.matmul(pC[2][0:1, ck * 512:(ck + 1) * 512], oc_f[:, :],
                                   r1[:, dc * 1024 + ck * 512:dc * 1024 + (ck + 1) * 512],
                                   start=(dc == 0), stop=(dc == 15))
            ins.then_inc(T.h, 1)
        sums_t = T.inc()

        sqmm = []
        for dc in range(16):
            @em("a")
            def _(a, dc=dc, w=segw(dc), wp=(sqmm[dc - 2] if dc >= 2 else None)):
                a.wait_ge(V.h, w)
                if wp is not None:
                    a.wait_ge(T.h, wp)
                a.activation(sq_buf[dc % 2][:, :], r1[:, dc * 1024:(dc + 1) * 1024], AF.Square)
                a.drain().then_inc(A.h, 1)
            sqa = A.inc()

            @em("t")
            def _(t, dc=dc, w=sqa):
                t.wait_ge(A.h, w)
                for ck in range(2):
                    ins = t.matmul(pC[3][0:1, ck * 512:(ck + 1) * 512], oc_f[:, :],
                                   sq_buf[dc % 2][:, ck * 512:(ck + 1) * 512],
                                   start=(dc == 0), stop=(dc == 15))
                ins.then_inc(T.h, 1)
            sqmm.append(T.inc())
        sq_t = sqmm[-1]

        @em("v")
        def _(v, w1=sums_t, w2=sq_t):
            v.wait_ge(T.h, w1)
            v.wait_ge(T.h, w2)
            v.tensor_scalar(row_mu[:, :], pC[2][0:1, 0:1024], float(1.0 / DM), None, ALU.mult)
            v.tensor_scalar(row_v[:, :], pC[3][0:1, 0:1024], float(1.0 / DM), None, ALU.mult)
            v.drain()
            v.tensor_tensor(row_t[:, :], row_mu[:, :], row_mu[:, :], ALU.mult)
            v.drain()
            v.tensor_tensor(row_v[:, :], row_v[:, :], row_t[:, :], ALU.subtract)
            v.drain()
            v.tensor_scalar(row_v[:, :], row_v[:, :], float(EPS), None, ALU.add)
            v.drain().then_inc(V.h, 1)
        var_v = V.inc()

        @em("a")
        def _(a, w=var_v):
            a.wait_ge(V.h, w)
            a.activation(row_t[:, :], row_v[:, :], AF.Sqrt, bias=0.0, scale=1.0)
            a.drain().then_inc(A.h, 1)
        std_a = A.inc()

        @em("v")
        def _(v, w=std_a):
            v.wait_ge(A.h, w)
            v.reciprocal(row_v[:, :], row_t[:, :])
            v.drain().then_inc(V.h, 1)
        rstd_v = V.inc()

        @em("t")
        def _(t, w=rstd_v):
            t.wait_ge(V.h, w)
            for ck in range(2):
                t.matmul(pC[2][:, ck * 512:(ck + 1) * 512], or_f[:, :],
                         row_mu[0:1, ck * 512:(ck + 1) * 512], start=True, stop=True)
            for ck in range(2):
                ins = t.matmul(pC[3][:, ck * 512:(ck + 1) * 512], or_f[:, :],
                               row_v[0:1, ck * 512:(ck + 1) * 512], start=True, stop=True)
            ins.then_inc(T.h, 1)
        bc_t = T.inc()

        @em("v")
        def _(v, w=bc_t):
            v.wait_ge(T.h, w)
            v.tensor_copy(mu_b[:, :], pC[2][:, :])
            v.tensor_copy(rs_b[:, :], pC[3][:, :])
            v.drain().then_inc(V.h, 1)
        bcv = V.inc()

        for dc in range(16):
            @em("v")
            def _(v, dc=dc, w=bcv, out_bf=out_bf):
                v.wait_ge(V.h, w)
                seg = r1[:, dc * 1024:(dc + 1) * 1024]
                v.tensor_tensor(seg, seg, mu_b[:, :], ALU.subtract)
                v.drain()
                if out_bf is not None:
                    v.tensor_tensor(out_bf[:, dc * 1024:(dc + 1) * 1024], seg, rs_b[:, :], ALU.mult)
                else:
                    v.tensor_tensor(seg, seg, rs_b[:, :], ALU.mult)
                v.drain().then_inc(V.h, 1)
            V.inc()
        return V.n

    ln1_done_v = layer_norm(r1_done, ln1bf, seg_waits=[r1_v[o] for o in range(16)])

    scope_f = ExitStack()
    w0t = scope_f.enter_context(nc.sbuf_tensor("w0t", [128, 16 * 1024], BF16))
    w1t = scope_f.enter_context(nc.sbuf_tensor("w1t", [128, 8 * 2048], BF16))
    dbuf = scope_f.enter_context(nc.sbuf_tensor("dbuf", [128, 8 * 1024], BF16))

    ffn1_t = {}
    gel_a = {}
    ffn2_t = {}
    eev_v = {}
    for grp in range(8):
        @em("s")
        def _(s, grp=grp, wt=wt_done, wv=r1_done,
              wprev=(ffn1_t[(grp - 1, 7)] if grp > 0 else None)):
            s.wait_ge(WS[0].h, wt[0])
            s.wait_ge(WS[1].h, wt[1])
            s.wait_ge(V.h, wv)
            if wprev is not None:
                s.wait_ge(T.h, wprev)
            s.dma_start(out=w0t[:, :].rearrange("p (dc n) -> p dc n", dc=16),
                        in_=w0t_dram.ap()[:, grp * 1024:(grp + 1) * 1024]
                        .rearrange("(dc p) n -> p dc n", p=128)).then_inc(SW0.h, 16)
        w0_sl = SW0.inc(16)

        @em("s")
        def _(s, grp=grp, wt=wt_done, wv=r1_done,
              wprev=(ffn2_t[(grp - 1, 15)] if grp > 0 else None)):
            s.wait_ge(WS[0].h, wt[0])
            s.wait_ge(WS[1].h, wt[1])
            s.wait_ge(V.h, wv)
            if wprev is not None:
                s.wait_ge(T.h, wprev)
            s.dma_start(out=w1t[:, :].rearrange("p (j h) -> p j h", j=8),
                        in_=w1t_dram.ap()[grp * 1024:(grp + 1) * 1024, :]
                        .rearrange("(j p) h -> p j h", p=128)).then_inc(SW1.h, 16)
        w1_sl = SW1.inc(16)

        for j in range(8):
            ps = pC[j % 2]
            if (grp, j - 2) in gel_a:
                wg = gel_a[(grp, j - 2)]
            elif grp > 0:
                wg = gel_a[(grp - 1, j + 6)]
            else:
                wg = None

            @em("t")
            def _(t, grp=grp, j=j, ps=ps, wl=w0_sl, wv=ln1_done_v, wg=wg):
                t.wait_ge(SW0.h, wl)
                t.wait_ge(V.h, wv)
                if wg is not None:
                    t.wait_ge(A.h, wg)
                for ck in range(2):
                    for dc in range(16):
                        ins = t.matmul(ps[:, ck * 512:(ck + 1) * 512],
                                       w0t[:, dc * 1024 + j * 128:dc * 1024 + j * 128 + 128],
                                       ln1bf[:, dc * 1024 + ck * 512:dc * 1024 + (ck + 1) * 512],
                                       start=(dc == 0), stop=(dc == 15))
                ins.then_inc(T.h, 1)
            ffn1_t[(grp, j)] = T.inc()

            @em("a")
            def _(a, grp=grp, j=j, ps=ps, w=ffn1_t[(grp, j)], nt=grp * 8 + j, wb=bias_v):
                a.wait_ge(T.h, w)
                a.wait_ge(V.h, wb)
                ins = a.activation(dbuf[:, j * 1024:(j + 1) * 1024], ps[:, :], AF.Gelu,
                                   bias=b0_sb[:, nt:nt + 1], scale=1.0)
                a.drain().then_inc(A.h, 1)
                del ins
            gel_a[(grp, j)] = A.inc()

        for ht in range(16):
            ps = pC[2 + ht % 2]
            if ht >= 2:
                we = eev_v[(grp, ht - 2)]
            elif grp > 0:
                we = eev_v[(grp - 1, 14 + ht)]
            else:
                we = None

            @em("t")
            def _(t, grp=grp, ht=ht, ps=ps, wg=gel_a[(grp, 7)], wl=w1_sl, we=we):
                t.wait_ge(A.h, wg)
                t.wait_ge(SW1.h, wl)
                if we is not None:
                    t.wait_ge(V.h, we)
                for ck in range(2):
                    for j in range(8):
                        ins = t.matmul(ps[:, ck * 512:(ck + 1) * 512],
                                       w1t[:, j * 2048 + ht * 128:j * 2048 + ht * 128 + 128],
                                       dbuf[:, j * 1024 + ck * 512:j * 1024 + (ck + 1) * 512],
                                       start=(j == 0), stop=(j == 7))
                ins.then_inc(T.h, 1)
            ffn2_t[(grp, ht)] = T.inc()

            @em("v")
            def _(v, grp=grp, ht=ht, ps=ps, w=ffn2_t[(grp, ht)], wsp=ln1_done_v):
                v.wait_ge(T.h, w)
                if grp == 0:
                    v.wait_ge(V.h, wsp)
                    v.tensor_copy(r1[:, ht * 1024:(ht + 1) * 1024], ps[:, :])
                else:
                    v.tensor_tensor(r1[:, ht * 1024:(ht + 1) * 1024],
                                    r1[:, ht * 1024:(ht + 1) * 1024], ps[:, :], ALU.add)
                v.drain().then_inc(V.h, 1)
            eev_v[(grp, ht)] = V.inc()

    scope_f.close()
    r2_v = {}
    for ht in range(16):
        @em("v")
        def _(v, ht=ht, we=eev_v[(7, 15)]):
            v.wait_ge(V.h, we)
            seg = r1[:, ht * 1024:(ht + 1) * 1024]
            v.scalar_tensor_tensor(seg, seg, b1_sb[:, ht:ht + 1],
                                   ln1bf[:, ht * 1024:(ht + 1) * 1024], ALU.add, ALU.add)
            v.drain().then_inc(V.h, 1)
        r2_v[ht] = V.inc()
    r2_done = V.n

    ln2_done_v = layer_norm(r2_done, None, seg_waits=[r2_v[h] for h in range(16)])

    scope_o = ExitStack()
    outT = [scope_o.enter_context(nc.sbuf_tensor(f"outT{i}", [128, 2048], F32)) for i in range(2)]

    otr_v = {}
    st_out = {}
    for tt in range(8):
        @em("t")
        def _(t, tt=tt, w=ln2_done_v, wprev=(otr_v[tt - 1] if tt > 0 else None)):
            t.wait_ge(V.h, w)
            if wprev is not None:
                t.wait_ge(V.h, wprev)
            for dc in range(16):
                ps = pC[dc // 8]
                ins = t.transpose(ps[:, (dc % 8) * 128:(dc % 8) * 128 + 128],
                                  r1[:, dc * 1024 + tt * 128:dc * 1024 + tt * 128 + 128],
                                  id_f[:, :])
            ins.then_inc(T.h, 1)
        ot = T.inc()

        @em("v")
        def _(v, tt=tt, w=ot, wst=(st_out[tt - 2] if tt > 1 else None)):
            v.wait_ge(T.h, w)
            if wst is not None:
                v.wait_ge(SO[tt % 2].h, wst)
            v.tensor_copy(outT[tt % 2][:, 0:1024], pC[0][:, :])
            v.tensor_copy(outT[tt % 2][:, 1024:2048], pC[1][:, :])
            v.drain().then_inc(V.h, 1)
        otr_v[tt] = V.inc()

        @em("s")
        def _(s, tt=tt, w=otr_v[tt]):
            s.wait_ge(V.h, w)
            s.dma_start(out=out_own.ap()[tt * 128:(tt + 1) * 128, :],
                        in_=outT[tt % 2][:, :]).then_inc(SO[tt % 2].h, 16)
        st_out[tt] = SO[tt % 2].inc(16)

    @em("s")
    def _(s, w6=st_out[6], w7=st_out[7]):
        s.wait_ge(SO[0].h, w6)
        s.wait_ge(SO[1].h, w7)

    @em("g")
    def _(g, w6=st_out[6], w7=st_out[7]):
        g.wait_ge(SO[0].h, w6)
        g.wait_ge(SO[1].h, w7)

    block = ctx.enter_context(nc.Block())

    @block.gpsimd
    def _(eng):
        for e, fn in prog:
            if e == "g":
                fn(eng)

    @block.sync
    def _(eng):
        for e, fn in prog:
            if e == "s":
                fn(eng)

    @block.tensor
    def _(eng):
        for e, fn in prog:
            if e == "t":
                fn(eng)

    @block.scalar
    def _(eng):
        for e, fn in prog:
            if e == "a":
                fn(eng)

    @block.vector
    def _(eng):
        for e, fn in prog:
            if e == "v":
                fn(eng)

    scope_o.close()
    scope_ln.close()
    scope_c.close()
    ctx.close()
    return nc


_NC_CACHE = None


def _get_nc():
    global _NC_CACHE
    if _NC_CACHE is None:
        _NC_CACHE = build_nc()
    return _NC_CACHE


def kernel(L0_attn_x, L0_attn_w_qkv, L0_attn_w_o, L0_ffn_W0, L0_ffn_bias0,
           L0_ffn_W1, L0_ffn_bias1):
    x = np.ascontiguousarray(np.asarray(L0_attn_x, dtype=np.float32).reshape(S, DM))
    wqkv = np.asarray(L0_attn_w_qkv, dtype=np.float32)
    wo = np.ascontiguousarray(np.asarray(L0_attn_w_o, dtype=np.float32))
    w0 = np.asarray(L0_ffn_W0, dtype=np.float32)
    b0 = np.ascontiguousarray(np.asarray(L0_ffn_bias0, dtype=np.float32))
    w1 = np.asarray(L0_ffn_W1, dtype=np.float32)
    b1 = np.ascontiguousarray(np.asarray(L0_ffn_bias1, dtype=np.float32))

    in_maps = []
    for c in range(NC):
        cols = []
        for h in (2 * c, 2 * c + 1):
            for wi in range(3):
                cols.append(wqkv[:, wi * DM + h * DH:wi * DM + (h + 1) * DH])
        in_maps.append({
            "x_own": np.ascontiguousarray(x[c * TOK:(c + 1) * TOK]),
            "wqkv_own": np.ascontiguousarray(np.concatenate(cols, axis=1)),
            "wo_in": wo,
            "w0_own": np.ascontiguousarray(w0[c * (NF // NC):(c + 1) * (NF // NC)]),
            "b0_in": b0,
            "w1_own": np.ascontiguousarray(w1[c * (DM // NC):(c + 1) * (DM // NC)]),
            "b1_in": b1,
        })

    nc = _get_nc()
    res = run_bass_kernel_spmd(nc, in_maps, core_ids=list(range(NC)))
    out = np.concatenate([np.asarray(r["out_own"], dtype=np.float32) for r in res.results], axis=0)
    return out.reshape(1, S, DM)
